# version 02
# baseline (speedup 1.0000x reference)
import numpy as np
import concourse.bass as bass
import concourse.mybir as mybir
from concourse.bass_utils import run_bass_kernel_spmd
from contextlib import ExitStack

F32 = mybir.dt.float32
BF16 = mybir.dt.bfloat16
I32 = mybir.dt.int32
AF = mybir.ActivationFunctionType
ALU = mybir.AluOpType
AX = mybir.AxisListType

D = 2048
DFF = 8192
NH, DH, HPC = 16, 128, 8
EPS = 1e-5
NEG = -1.0e30
C_SCALE = float(1.0 / np.sqrt(DH).astype(np.float32))

ENGS = ("sync", "tensor", "vector", "scalar", "gpsimd")
import os as _os
SELFWAIT = _os.environ.get("K_SELFWAIT") == "1"
RG = [[0, 1], [2, 3], [4, 5], [6, 7]]


class Sem:
    def __init__(self, handle, name):
        self.h, self.name, self.n = handle, name, 0

    def inc(self, by=1):
        self.n += by
        return self.n


class Sched:

    def __init__(self, nc, stack):
        self.nc = nc
        self.ops = {e: [] for e in ENGS}
        self.last_wait = {e: {} for e in ENGS}
        self._stack = stack
        self._sems = {}

    def sem(self, name):
        if name not in self._sems:
            h = self._stack.enter_context(self.nc.semaphore(name))
            self._sems[name] = Sem(h, name)
        return self._sems[name]

    def wait(self, eng, sem, val):
        if val <= 0:
            return
        lw = self.last_wait[eng]
        if lw.get(sem.name, -1) >= val:
            return
        lw[sem.name] = val
        self.ops[eng].append(lambda e, h=sem.h, v=val: e.wait_ge(h, v))

    def emit(self, eng, fn, inc=None, inc_by=1):
        if inc is not None:
            if eng in ("vector", "scalar", "gpsimd"):
                self.wait(eng, inc, inc.n)
            v = inc.inc(inc_by)
            def wrapped(e, fn=fn, h=inc.h, by=inc_by):
                ins = fn(e)
                assert ins is not None, "emitter must return its last instruction"
                ins.then_inc(h, by)
            self.ops[eng].append(wrapped)
            return v
        self.ops[eng].append(fn)
        return None

    def dma(self, eng, out, in_, inc):
        if isinstance(inc, str):
            inc = self.sem("dma_" + inc)
        v = inc.inc(16)
        def fn(e, out=out, in_=in_, h=inc.h):
            e.dma_start(out=out, in_=in_).then_inc(h, 16)
        self.ops[eng].append(fn)
        return v

    def dwait(self, eng, key, val):
        self.wait(eng, self.sem("dma_" + key), val)

    def barrier(self, name):
        snap = [(s, s.n) for s in self._sems.values() if s.n > 0]
        for e in ENGS:
            for s, v in snap:
                self.wait(e, s, v)


class _StopBuild(Exception):
    pass


def build_program(T=4096, debug=False, phases=99):
    import os
    EN_TR = os.environ.get("K_NO_TR") != "1"
    EN_WS = os.environ.get("K_NO_WS") != "1"
    EN_SP = os.environ.get("K_NO_SP") != "1"
    K_NTS = int(os.environ.get("K_NTS", "99"))
    NTT = T // 128
    NTS = T // 512
    NQT = NTS
    OWN = T // 2
    NOT = OWN // 128
    NOJ = OWN // 512

    nc = bass.Bass("TRN2", target_bir_lowering=False,
                   detect_race_conditions=_os.environ.get("K_RACEDET") == "1")

    ikind = "ExternalOutput" if debug else "Internal"

    x_in = nc.dram_tensor("x", [T, D], F32, kind="ExternalInput").ap()
    wq_in = nc.dram_tensor("wq", [D, D], F32, kind="ExternalInput").ap()
    xq_in = nc.dram_tensor("xq", [OWN, D], F32, kind="ExternalInput").ap()
    wk_in = nc.dram_tensor("wk", [D, HPC * DH], F32, kind="ExternalInput").ap()
    wv_in = nc.dram_tensor("wv", [D, HPC * DH], F32, kind="ExternalInput").ap()
    bq_in = nc.dram_tensor("bq", [128, 16], F32, kind="ExternalInput").ap()
    bk_in = nc.dram_tensor("bk", [128, HPC], F32, kind="ExternalInput").ap()
    bv_in = nc.dram_tensor("bv", [1, HPC * DH], F32, kind="ExternalInput").ap()
    wo_in = nc.dram_tensor("wo", [D, D], F32, kind="ExternalInput").ap()
    bo_in = nc.dram_tensor("bo", [1, D], F32, kind="ExternalInput").ap()
    w1t_in = nc.dram_tensor("w1t", [D, DFF], F32, kind="ExternalInput").ap()
    w2t_in = nc.dram_tensor("w2t", [DFF, D], F32, kind="ExternalInput").ap()
    masks_in = nc.dram_tensor("masks", [4, 128, 512], F32, kind="ExternalInput").ap()
    identb_in = nc.dram_tensor("identb", [128, 128], BF16, kind="ExternalInput").ap()
    y_out = nc.dram_tensor("y", [OWN, D], F32, kind="ExternalOutput").ap()

    xT_sp = nc.dram_tensor("xT_sp", [2, 16, 128, T], BF16, kind=ikind).ap()
    qT_sp = nc.dram_tensor("qT_sp", [2, 16, 128, OWN], BF16, kind=ikind).ap()
    kT_sp = nc.dram_tensor("kT_sp", [2, HPC, 128, T], BF16, kind=ikind).ap()
    v_sp = nc.dram_tensor("v_sp", [2, T, HPC, 132], BF16, kind=ikind).ap()
    wo_sp = nc.dram_tensor("wo_sp", [2, 16, 128, D], BF16, kind="Internal").ap()
    tern2_sp = nc.dram_tensor("tern2_sp", [64, 128, D], BF16, kind=ikind).ap()
    xmid_sp = nc.dram_tensor("xmid_sp", [OWN, D], F32, kind=ikind).ap()
    h_sp = nc.dram_tensor("h_sp", [OWN, DFF], F32, kind=ikind).ap()
    ag_in = nc.dram_tensor("ag_in", [NQT, 2, HPC, 128, 512], BF16, kind="Internal").ap()
    ag_out = nc.dram_tensor("ag_out", [NQT, 2, 2, HPC, 128, 512], BF16, kind="Internal").ap()
    if debug:
        dbg_agin = nc.dram_tensor("dbg_agin", [NQT, 2, HPC, 128, 512], BF16,
                                  kind="ExternalOutput").ap()
        dbg_xint = nc.dram_tensor("dbg_xint", [OWN, D], BF16, kind="ExternalOutput").ap()
        dbg_wsc = nc.dram_tensor("dbg_wsc", [128, 6], F32, kind="ExternalOutput").ap()
        dbg_p = nc.dram_tensor("dbg_p", [128, 512], F32, kind="ExternalOutput").ap()
        dbg_sm = nc.dram_tensor("dbg_sm", [128, 512], F32, kind="ExternalOutput").ap()
        dbg_mt = nc.dram_tensor("dbg_mt", [128, HPC], F32, kind="ExternalOutput").ap()
        dbg_av = nc.dram_tensor("dbg_av", [128, 516], F32, kind="ExternalOutput").ap()

    AG_QT_STRIDE = 2 * 2 * HPC * 128 * 512

    with ExitStack() as stack:
        ent = stack.enter_context
        sc = Sched(nc, stack)

        masks = ent(nc.sbuf_tensor("masks_sb", [128, 4, 512], F32))
        identb = ent(nc.sbuf_tensor("identb_sb", [128, 128], BF16))
        ones_row = ent(nc.sbuf_tensor("ones_row", [1, 512], F32))
        ones_col = ent(nc.sbuf_tensor("ones_col", [128, 1], F32))
        bq_sb = ent(nc.sbuf_tensor("bq_sb", [128, 16], F32))
        bk_sb = ent(nc.sbuf_tensor("bk_sb", [128, HPC], F32))
        bv_sb = ent(nc.sbuf_tensor("bv_sb", [1, HPC * DH], F32))
        bo_sb = ent(nc.sbuf_tensor("bo_sb", [1, D], F32))
        ln1 = ent(nc.sbuf_tensor("ln1", [128, 2 * NTT], F32))
        ln2s = ent(nc.sbuf_tensor("ln2s", [128, 4 * NOT], F32))
        qs_sb = ent(nc.sbuf_tensor("qs_sb", [128, 2 * NOT], F32))
        hs_sb = ent(nc.sbuf_tensor("hs_sb", [128, 3 * NOT], F32))
        wsc = ent(nc.sbuf_tensor("wsc", [128, 6], F32))
        wacc = ent(nc.sbuf_tensor("wacc", [128, 2], F32))
        wsca = ent(nc.sbuf_tensor("wsca", [1, 16], F32))
        nmx = ent(nc.sbuf_tensor("nmx", [1, 2 * HPC + 2], F32))
        mtil = ent(nc.sbuf_tensor("mtil", [128, HPC], F32))
        stat_s1 = ent(nc.sbuf_tensor("stat_s1", [128, 1], F32))
        stat_sq = ent(nc.sbuf_tensor("stat_sq", [128, 1], F32))
        stat_wq = ent(nc.sbuf_tensor("stat_wq", [128, 1], F32))
        epsc = ent(nc.sbuf_tensor("epsc", [128, 1], F32))
        sq_scr = ent(nc.sbuf_tensor("sq_scr", [128, 2048], F32))
        tiny = ent(nc.sbuf_tensor("tiny", [128, 8], F32))

        pS0 = ent(nc.psum_tensor("pS0", [128, 512], F32))
        pS1 = ent(nc.psum_tensor("pS1", [128, 512], F32))
        pA0 = ent(nc.psum_tensor("pA0", [128, 512], F32))
        pA1 = ent(nc.psum_tensor("pA1", [128, 512], F32))
        pO0 = ent(nc.psum_tensor("pO0", [128, 512], F32))
        pO1 = ent(nc.psum_tensor("pO1", [128, 512], F32))
        pT0 = ent(nc.psum_tensor("pT0", [128, 1024], BF16))
        pT1 = ent(nc.psum_tensor("pT1", [128, 1024], BF16))
        pT = [pT0, pT1]
        pS = [pS0, pS1]
        pO = [pO0, pO1]
        pA = [pA0, pA1]

        s_ld = sc.sem("s_ld")
        s_g = sc.sem("s_g")
        s_v = sc.sem("s_v")
        s_s = sc.sem("s_s")
        s_t = sc.sem("s_t")
        s_gp = sc.sem("s_gp")
        s_cc = sc.sem("s_cc")
        s_st = sc.sem("s_st")

        try:
            _ph_try = True
            dyn = {}
            dyn_s = {}
            def _mkdyn_s(e):
                cid = e.partition_id()
                dyn_s["xtok"] = (cid % 2) * OWN
            sc.ops["sync"].append(_mkdyn_s)
            def _mkdyn(e):
                cid = e.partition_id()
                half = cid % 2
                dyn["ag"] = half * (NOJ * AG_QT_STRIDE)
                dyn["x"] = half * (OWN * D)
            sc.ops["gpsimd"].append(_mkdyn)

            for _mi in range(4):
                sc.dma("sync", masks[:, _mi, :], masks_in[_mi], "p0")
            sc.dma("sync", identb[:], identb_in, "p0")
            sc.dma("sync", bq_sb[:], bq_in, "p0")
            sc.dma("sync", bk_sb[:], bk_in, "p0")
            sc.dma("sync", bv_sb[:], bv_in, "p0")
            p0v = sc.dma("sync", bo_sb[:], bo_in, "p0")
            sc.dwait("vector", "p0", p0v)
            def init_v(e):
                e.memset(ones_row[:], 1.0)
                e.memset(epsc[:], EPS)
                e.memset(ones_col[:], 1.0)
                e.memset(wacc[:], 0.0)
                e.memset(nmx[:], 0.0)
                e.memset(ln2s[:], 0.0)
                return e.memset(hs_sb[:], 0.0)
            sc.emit("vector", init_v, inc=s_v)
            init_done = s_v.n
            sc.barrier("p0")
            if phases <= 0:
                raise _StopBuild

            with ExitStack() as ph:
                xa = ph.enter_context(nc.sbuf_tensor("xa", [128, D], F32))
                xb = ph.enter_context(nc.sbuf_tensor("xb", [128, D], F32))
                xhat = ph.enter_context(nc.sbuf_tensor("xhat", [128, D], F32))
                xhi = ph.enter_context(nc.sbuf_tensor("xhi", [128, D], BF16))
                xlo = ph.enter_context(nc.sbuf_tensor("xlo", [128, D], BF16))
                xTh_a = ph.enter_context(nc.sbuf_tensor("xTh_a", [128, 16, 512], BF16))
                xTl_a = ph.enter_context(nc.sbuf_tensor("xTl_a", [128, 16, 512], BF16))
                xTh_b = ph.enter_context(nc.sbuf_tensor("xTh_b", [128, 16, 512], BF16))
                xTl_b = ph.enter_context(nc.sbuf_tensor("xTl_b", [128, 16, 512], BF16))
                wsa = ph.enter_context(nc.sbuf_tensor("wsa", [128, 2048], F32))
                wsb = ph.enter_context(nc.sbuf_tensor("wsb", [128, 2048], F32))

                sc.wait("vector", s_v, init_done)
                sc.wait("scalar", s_v, init_done)

                wstats = []
                for r in range(16):
                    for ctile in range(4):
                        wstats.append((w1t_in[r * 128:(r + 1) * 128, ctile * 2048:(ctile + 1) * 2048], 0))
                for r in range(64):
                    wstats.append((w2t_in[r * 128:(r + 1) * 128, :], 1))
                wstats_i = 0
                wst_ld = {0: 0, 1: 0}
                wst_rd = {0: 0, 1: 0}
                wst_last_add = 0

                x_ld = {0: 0, 1: 0}
                x_rd = {0: 0, 1: 0}
                tr_ev = {0: 0, 1: 0}
                sp_done = {0: 0, 1: 0}

                n_wstats_per_tt = (len(wstats) + NTT - 1) // NTT

                for ts in range(min(NTS, K_NTS)):
                    xTh = [xTh_a, xTh_b][ts % 2]
                    xTl = [xTl_a, xTl_b][ts % 2]
                    sc.dwait("sync", f"xsp{ts % 2}", sp_done[ts % 2])
                    for j in range(4):
                        tt = ts * 4 + j
                        xt = [xa, xb][tt % 2]
                        bi = tt % 2
                        sc.wait("sync", s_s, x_rd[bi])
                        x_ld[bi] = sc.dma("sync", xt[:], x_in[tt * 128:(tt + 1) * 128, :], f"xld{bi}")
                        sc.dwait("vector", f"xld{bi}", x_ld[bi])
                        sc.emit("vector", lambda e, xt=xt: e.reduce_sum(stat_s1[:], xt[:], AX.X), inc=s_v)
                        s1v = s_v.n
                        sc.dwait("scalar", f"xld{bi}", x_ld[bi])
                        sc.emit("scalar", lambda e, xt=xt: e.activation(sq_scr[:], xt[:], AF.Square,
                                                                        accum_out=stat_sq[:]), inc=s_s)
                        sqv = s_s.n
                        sc.wait("vector", s_s, sqv)
                        def lnst1(e):
                            e.tensor_scalar(tiny[:, 0:1], stat_s1[:], 1.0 / D, 0.0, ALU.mult, ALU.add)
                            return e.tensor_scalar(tiny[:, 1:2], stat_sq[:], 1.0 / D, 0.0, ALU.mult, ALU.add)
                        sc.emit("vector", lnst1, inc=s_v)
                        sc.emit("vector", lambda e: e.tensor_tensor(tiny[:, 2:3], tiny[:, 0:1],
                                                                    tiny[:, 0:1], ALU.mult), inc=s_v)
                        sc.emit("vector", lambda e: e.tensor_tensor(tiny[:, 3:4], tiny[:, 1:2],
                                                                    tiny[:, 2:3], ALU.subtract), inc=s_v)
                        varv = s_v.n
                        sc.wait("scalar", s_v, varv)
                        sc.emit("scalar", lambda e: e.activation(tiny[:, 4:5], tiny[:, 3:4], AF.Sqrt,
                                                                 bias=epsc[:]), inc=s_s)
                        sdv = s_s.n
                        sc.wait("vector", s_s, sdv)
                        sc.emit("vector", lambda e, tt=tt: e.reciprocal(
                            ln1[:, 2 * tt:2 * tt + 1], tiny[:, 4:5]), inc=s_v)
                        sc.emit("vector", lambda e, tt=tt: e.scalar_tensor_tensor(
                            ln1[:, 2 * tt + 1:2 * tt + 2], tiny[:, 0:1], -1.0,
                            ln1[:, 2 * tt:2 * tt + 1], ALU.mult, ALU.mult), inc=s_v)
                        rsv = s_v.n
                        sc.wait("scalar", s_v, rsv)
                        sc.emit("scalar", lambda e, xt=xt, tt=tt: e.activation(
                            xhat[:], xt[:], AF.Identity, bias=ln1[:, 2 * tt + 1:2 * tt + 2],
                            scale=ln1[:, 2 * tt:2 * tt + 1]), inc=s_s)
                        x_rd[bi] = s_s.n
                        xhv = s_s.n
                        sc.wait("vector", s_s, xhv)
                        sc.emit("vector", lambda e: e.tensor_copy(xhi[:], xhat[:]), inc=s_v)
                        hiv = s_v.n
                        sc.wait("gpsimd", s_v, hiv)
                        sc.emit("gpsimd", lambda e: e.tensor_tensor(xlo[:], xhat[:], xhi[:], ALU.subtract), inc=s_gp)
                        lov = s_gp.n
                        sc.wait("tensor", s_gp, lov)
                        for dc in (range(16) if EN_TR else []):
                            bk = dc % 2
                            sc.wait("tensor", s_v, tr_ev[bk])
                            def trf(e, pb=pT[bk], dc=dc):
                                e.transpose(pb[:, 0:128], xhi[:, dc * 128:(dc + 1) * 128], identb[:])
                                return e.transpose(pb[:, 128:256], xlo[:, dc * 128:(dc + 1) * 128], identb[:])
                            sc.emit("tensor", trf, inc=s_t)
                            trv = s_t.n
                            sc.wait("vector", s_t, trv)
                            def evf(e, pb=pT[bk], dc=dc, j=j, xTh=xTh, xTl=xTl):
                                e.tensor_copy(xTh[:, dc, j * 128:(j + 1) * 128], pb[:, 0:128])
                                return e.tensor_copy(xTl[:, dc, j * 128:(j + 1) * 128], pb[:, 128:256])
                            sc.emit("vector", evf, inc=s_v)
                            tr_ev[bk] = s_v.n
                        for k in (range(n_wstats_per_tt) if EN_WS else []):
                            if wstats_i >= len(wstats):
                                break
                            src, part = wstats[wstats_i]
                            wb = wstats_i % 2
                            wt = [wsa, wsb][wb]
                            sc.wait("sync", s_v, wst_rd[wb])
                            wst_ld[wb] = sc.dma("sync", wt[:, 0:src.shape[1]], src, f"wst{wb}")
                            sc.dwait("scalar", f"wst{wb}", wst_ld[wb])
                            sc.wait("scalar", s_v, wst_last_add)
                            sc.emit("scalar", lambda e, wt=wt, n=src.shape[1]: e.activation(
                                sq_scr[:, 0:n], wt[:, 0:n], AF.Square, accum_out=stat_wq[:]), inc=s_s)
                            wsv = s_s.n
                            sc.wait("vector", s_s, wsv)
                            sc.emit("vector", lambda e, part=part: e.tensor_tensor(
                                wacc[:, part:part + 1], wacc[:, part:part + 1], stat_wq[:], ALU.add), inc=s_v)
                            wst_rd[wb] = s_v.n
                            wst_last_add = s_v.n
                            wstats_i += 1
                    spv = s_v.n
                    if not (EN_SP and EN_TR):
                        continue
                    sc.wait("sync", s_v, spv)
                    sc.dma("sync", xT_sp[0, :, :, ts * 512:(ts + 1) * 512].transpose([1, 0, 2]), xTh[:],
                           f"xsp{ts % 2}")
                    sp_done[ts % 2] = sc.dma("sync",
                                             xT_sp[1, :, :, ts * 512:(ts + 1) * 512].transpose([1, 0, 2]),
                                             xTl[:], f"xsp{ts % 2}")
                assert (not EN_WS) or K_NTS < 99 or wstats_i == len(wstats)

            sc.barrier("pa1")
            if phases <= 1:
                raise _StopBuild

            def wred(e):
                return e.matmul(pS0[0:1, 0:2], ones_col[:], wacc[:, 0:2], start=True, stop=True)
            sc.emit("tensor", wred, inc=s_t)
            wrv = s_t.n
            sc.wait("vector", s_t, wrv)
            def wfin(e):
                return e.tensor_scalar(wsca[0:1, 0:2], pS0[0:1, 0:2], 1.0 / (D * DFF), EPS, ALU.mult, ALU.max)
            sc.emit("vector", wfin, inc=s_v)
            wfv = s_v.n
            sc.wait("scalar", s_v, wfv)
            sc.emit("scalar", lambda e: e.activation(wsca[0:1, 2:4], wsca[0:1, 0:2], AF.Sqrt), inc=s_s)
            sq2v = s_s.n
            sc.wait("vector", s_s, sq2v)
            sc.emit("vector", lambda e: e.reciprocal(wsca[0:1, 4:6], wsca[0:1, 2:4]), inc=s_v)
            sc.emit("vector", lambda e: e.tensor_tensor(wsca[0:1, 4:6], wsca[0:1, 0:2],
                                                        wsca[0:1, 4:6], ALU.mult), inc=s_v)
            sc.emit("vector", lambda e: e.tensor_tensor(wsca[0:1, 4:6], wsca[0:1, 2:4],
                                                        wsca[0:1, 4:6], ALU.add), inc=s_v)
            sc.emit("vector", lambda e: e.tensor_scalar(wsca[0:1, 2:4], wsca[0:1, 4:6], 0.5, 0.0,
                                                        ALU.mult, ALU.add), inc=s_v)
            sc.emit("vector", lambda e: e.tensor_scalar(wsca[0:1, 4:6], wsca[0:1, 2:4], 0.5, 0.0,
                                                        ALU.mult, ALU.add), inc=s_v)
            sc.emit("vector", lambda e: e.tensor_scalar(wsca[0:1, 6:8], wsca[0:1, 4:6], -1.0, 0.0,
                                                        ALU.mult, ALU.add), inc=s_v)
            ntv = s_v.n
            sc.wait("tensor", s_v, ntv)
            def wbc(e):
                return e.matmul(pS0[:, 0:8], ones_row[0:1, 0:128], wsca[0:1, 0:8], start=True, stop=True)
            sc.emit("tensor", wbc, inc=s_t)
            wbv = s_t.n
            sc.wait("vector", s_t, wbv)
            def wscf(e):
                e.tensor_copy(wsc[:, 0:1], pS0[:, 4:5])
                e.tensor_copy(wsc[:, 1:2], pS0[:, 6:7])
                e.tensor_copy(wsc[:, 2:3], pS0[:, 5:6])
                e.tensor_copy(wsc[:, 3:4], pS0[:, 7:8])
                e.tensor_copy(wsc[:, 4:5], pS0[:, 2:3])
                return e.tensor_copy(wsc[:, 5:6], pS0[:, 3:4])
            sc.emit("vector", wscf, inc=s_v)
            wsfv = s_v.n
            if debug:
                sc.wait("sync", s_v, wsfv)
                sc.dma("sync", dbg_wsc, wsc[:], "dbgw")
            sc.wait("tensor", s_v, wsfv)

            with ExitStack() as ph:
                wf_c = [ph.enter_context(nc.sbuf_tensor(f"wf_c{i}", [128, 2, 1024], F32)) for i in range(2)]
                whi = ph.enter_context(nc.sbuf_tensor("whi", [128, 16, 1024], BF16))
                wlo = ph.enter_context(nc.sbuf_tensor("wlo", [128, 16, 1024], BF16))
                xs_h = [ph.enter_context(nc.sbuf_tensor(f"xs_h{i}", [128, 16, 512], BF16)) for i in range(2)]
                xs_l = [ph.enter_context(nc.sbuf_tensor(f"xs_l{i}", [128, 16, 512], BF16)) for i in range(2)]
                qev_h = [ph.enter_context(nc.sbuf_tensor(f"qev_h{i}", [128, 512], BF16)) for i in range(2)]
                qev_l = [ph.enter_context(nc.sbuf_tensor(f"qev_l{i}", [128, 512], BF16)) for i in range(2)]
                vev_h = [ph.enter_context(nc.sbuf_tensor(f"vev_h{i}", [128, 8, 132], BF16)) for i in range(2)]
                vev_l = [ph.enter_context(nc.sbuf_tensor(f"vev_l{i}", [128, 8, 132], BF16)) for i in range(2)]
                nsq = ph.enter_context(nc.sbuf_tensor("nsq", [128, 512], F32))

                xs_rd = {0: 0, 1: 0}
                qk_bank_rd = {0: 0, 1: 0}
                vp_bank_rd = {0: 0, 1: 0}
                qev_st = {0: 0, 1: 0}
                vev_st = {0: 0, 1: 0}
                w_rd = 0
                nsq_rd = 0
                pnorm_rd = 0

                stages = [
                    (wq_in[:, 0:1024], qT_sp, bq_sb, False, True, 0),
                    (wq_in[:, 1024:2048], qT_sp, bq_sb, False, True, 8),
                    (wk_in, kT_sp, bk_sb, False, False, 0),
                    (wv_in, v_sp, bv_sb, True, False, 0),
                ]
                for wi, (w_in, spdst, bias_sb, token_major, own_only, fcb) in enumerate(stages):
                    sc.wait("sync", s_t, w_rd)
                    wf_rd = {0: 0, 1: 0}
                    for wch in range(8):
                        wbi = wch % 2
                        sc.wait("sync", s_gp, wf_rd[wbi])
                        wldv = sc.dma("sync", wf_c[wbi][:],
                                      w_in[wch * 256:(wch + 1) * 256, :]
                                      .rearrange("(a p) f -> p a f", p=128), f"wfc{wbi}")
                        sc.dwait("vector", f"wfc{wbi}", wldv)
                        sc.wait("vector", s_t, w_rd)
                        sc.wait("gpsimd", s_t, w_rd)
                        sc.emit("vector", lambda e, b=wbi, wch=wch: e.tensor_copy(
                            whi[:, 2 * wch:2 * wch + 2, :].rearrange("p a b -> p (a b)"),
                            wf_c[b][:].rearrange("p a b -> p (a b)")), inc=s_v)
                        whiv = s_v.n
                        sc.wait("gpsimd", s_v, whiv)
                        sc.dwait("gpsimd", f"wfc{wbi}", wldv)
                        sc.emit("gpsimd", lambda e, b=wbi, wch=wch: e.tensor_tensor(
                            wlo[:, 2 * wch:2 * wch + 2, :].rearrange("p a b -> p (a b)"),
                            wf_c[b][:].rearrange("p a b -> p (a b)"),
                            whi[:, 2 * wch:2 * wch + 2, :].rearrange("p a b -> p (a b)"),
                            ALU.subtract), inc=s_gp)
                        wf_rd[wbi] = s_gp.n
                    wlov = s_gp.n
                    sc.wait("tensor", s_gp, wlov)

                    if token_major:
                        vp_bank_rd[0] = max(vp_bank_rd[0], pnorm_rd)
                        vp_bank_rd[1] = max(vp_bank_rd[1], pnorm_rd)
                    for ts in range(NTS // 2 if own_only else NTS):
                        bi = ts % 2
                        xh, xl = xs_h[bi], xs_l[bi]
                        sc.wait("sync", s_t, xs_rd[bi])
                        src0 = xT_sp[0, :, :, ts * 512:(ts + 1) * 512].transpose([1, 0, 2])
                        src1 = xT_sp[1, :, :, ts * 512:(ts + 1) * 512].transpose([1, 0, 2])
                        if own_only:
                            def _qld(e, src=src0, dst=xh, key="xts%d" % bi):
                                s2 = bass.AP(src.tensor, src.offset + dyn_s["xtok"], src.ap)
                                e.dma_start(out=dst[:], in_=s2).then_inc(
                                    sc.sem("dma_" + key).h, 16)
                            sc.sem("dma_xts%d" % bi).inc(16)
                            sc.ops["sync"].append(_qld)
                            def _qld2(e, src=src1, dst=xl, key="xts%d" % bi):
                                s2 = bass.AP(src.tensor, src.offset + dyn_s["xtok"], src.ap)
                                e.dma_start(out=dst[:], in_=s2).then_inc(
                                    sc.sem("dma_" + key).h, 16)
                            sc.sem("dma_xts%d" % bi).inc(16)
                            sc.ops["sync"].append(_qld2)
                            xldv = sc.sem("dma_xts%d" % bi).n
                        else:
                            sc.dma("sync", xh[:], src0, f"xts{bi}")
                            xldv = sc.dma("sync", xl[:], src1, f"xts{bi}")
                        sc.dwait("tensor", f"xts{bi}", xldv)
                        if not token_major:
                            for fc in range(8):
                                bk = fc % 2
                                pb = pS[bk]
                                sc.wait("tensor", s_v, qk_bank_rd[bk])
                                def qkmm(e, pb=pb, fc=fc, xh=xh, xl=xl):
                                    for dc in range(16):
                                        e.matmul(pb[:], whi[:, dc, fc * 128:(fc + 1) * 128], xh[:, dc, :],
                                                 start=(dc == 0), stop=False)
                                        e.matmul(pb[:], whi[:, dc, fc * 128:(fc + 1) * 128], xl[:, dc, :],
                                                 start=False, stop=False)
                                        ins = e.matmul(pb[:], wlo[:, dc, fc * 128:(fc + 1) * 128],
                                                       xh[:, dc, :], start=False, stop=(dc == 15))
                                    return ins
                                sc.emit("tensor", qkmm, inc=s_t)
                                mmv = s_t.n
                                xs_rd[bi] = mmv
                                w_rd = mmv
                                ebi = fc % 2
                                sc.wait("scalar", s_t, mmv)
                                sc.dwait("scalar", f"qsp{ebi}", qev_st[ebi])
                                sc.emit("scalar", lambda e, pb=pb, fc=fcb + fc, bias_sb=bias_sb,
                                        qh=qev_h[ebi]: e.activation(qh[:], pb[:], AF.Identity,
                                                                    bias=bias_sb[:, fc:fc + 1]),
                                        inc=s_s)
                                hiv = s_s.n
                                sc.wait("scalar", s_v, pnorm_rd)
                                sc.wait("scalar", s_t, nsq_rd)
                                sc.emit("scalar", lambda e, qh=qev_h[ebi]: e.activation(
                                    nsq[:], qh[:], AF.Square), inc=s_s)
                                nqv = s_s.n
                                sc.wait("vector", s_s, hiv)
                                sc.emit("vector", lambda e, pb=pb, fc=fcb + fc, bias_sb=bias_sb,
                                        qh=qev_h[ebi], ql=qev_l[ebi]: e.scalar_tensor_tensor(
                                            ql[:], pb[:], bias_sb[:, fc:fc + 1], qh[:],
                                            ALU.add, ALU.subtract), inc=s_v)
                                qk_bank_rd[bk] = s_v.n
                                sc.wait("tensor", s_s, nqv)
                                sc.wait("tensor", s_v, pnorm_rd)
                                def nmm(e):
                                    return e.matmul(pO0[0:1, 0:512], ones_col[:], nsq[:], start=True, stop=True)
                                sc.emit("tensor", nmm, inc=s_t)
                                nsq_rd = s_t.n
                                nmv = s_t.n
                                sc.wait("vector", s_t, nmv)
                                sc.emit("vector", lambda e: e.reduce_max(
                                    wsca[0:1, 8:9], pO0[0:1, 0:512], AX.X), inc=s_v)
                                sc.emit("vector", lambda e, col=(16 if own_only else fc):
                                        e.tensor_tensor(
                                            nmx[0:1, col:col + 1], nmx[0:1, col:col + 1],
                                            wsca[0:1, 8:9], ALU.max), inc=s_v)
                                pnorm_rd = s_v.n
                                lov = s_v.n
                                sc.wait("sync", s_v, qk_bank_rd[bk])
                                sc.dma("sync", spdst[0, fcb + fc, :, ts * 512:(ts + 1) * 512],
                                       qev_h[ebi][:], f"qsp{ebi}")
                                qev_st[ebi] = sc.dma("sync",
                                                     spdst[1, fcb + fc, :, ts * 512:(ts + 1) * 512],
                                                     qev_l[ebi][:], f"qsp{ebi}")
                        else:
                            for sub in range(4):
                                vbi = sub % 2
                                sc.dwait("scalar", f"vsp{vbi}", vev_st[vbi])
                                for fv in range(2):
                                    bk = fv
                                    pb = pO[bk]
                                    sc.wait("tensor", s_v, vp_bank_rd[bk])
                                    def vmm(e, pb=pb, sub=sub, fv=fv, xh=xh, xl=xl, bias_sb=bias_sb):
                                        for dc in range(16):
                                            e.matmul(pb[:], xh[:, dc, sub * 128:(sub + 1) * 128],
                                                     whi[:, dc, fv * 512:(fv + 1) * 512],
                                                     start=(dc == 0), stop=False)
                                            e.matmul(pb[:], xh[:, dc, sub * 128:(sub + 1) * 128],
                                                     wlo[:, dc, fv * 512:(fv + 1) * 512], start=False, stop=False)
                                            e.matmul(pb[:], xl[:, dc, sub * 128:(sub + 1) * 128],
                                                     whi[:, dc, fv * 512:(fv + 1) * 512], start=False, stop=False)
                                        return e.matmul(pb[:], ones_row[0:1, 0:128],
                                                 bias_sb[0:1, fv * 512:(fv + 1) * 512], start=False, stop=True)
                                    sc.emit("tensor", vmm, inc=s_t)
                                    mmv = s_t.n
                                    xs_rd[bi] = mmv
                                    w_rd = mmv
                                    sc.wait("scalar", s_t, mmv)
                                    def vevh_f(e, pb=pb, fv=fv, vh=vev_h[vbi]):
                                        return e.copy(vh[:, fv * 4:(fv + 1) * 4, 0:128],
                                               pb[:].rearrange("p (a b) -> p a b", a=4))
                                    sc.emit("scalar", vevh_f, inc=s_s)
                                    hiv = s_s.n
                                    sc.wait("vector", s_s, hiv)
                                    def vevl_f(e, pb=pb, fv=fv, vh=vev_h[vbi], vl=vev_l[vbi]):
                                        return e.tensor_tensor(vl[:, fv * 4:(fv + 1) * 4, 0:128],
                                                        pb[:].rearrange("p (a b) -> p a b", a=4),
                                                        vh[:, fv * 4:(fv + 1) * 4, 0:128], ALU.subtract)
                                    sc.emit("vector", vevl_f, inc=s_v)
                                    vp_bank_rd[bk] = s_v.n
                                def vones(e, vh=vev_h[vbi], vl=vev_l[vbi]):
                                    e.memset(vh[:, :, 128:132], 1.0)
                                    return e.memset(vl[:, :, 128:132], 0.0)
                                sc.emit("vector", vones, inc=s_v)
                                ov = s_v.n
                                sc.wait("sync", s_v, ov)
                                tt = ts * 4 + sub
                                sc.dma("sync", v_sp[0, tt * 128:(tt + 1) * 128, :, 0:132], vev_h[vbi][:],
                                       f"vsp{vbi}")
                                vev_st[vbi] = sc.dma("sync", v_sp[1, tt * 128:(tt + 1) * 128, :, 0:132],
                                                     vev_l[vbi][:], f"vsp{vbi}")

                mxv = s_v.n
                sc.wait("vector", s_v, mxv)
                def mt1(e):
                    return e.tensor_single_scalar(wsca[0:1, 8:8 + HPC], nmx[0:1, 0:HPC],
                                                  nmx[0:1, 16:17], ALU.mult)
                sc.emit("vector", mt1, inc=s_v)
                m1v = s_v.n
                sc.wait("scalar", s_v, m1v)
                sc.emit("scalar", lambda e: e.activation(wsca[0:1, 8:8 + HPC], wsca[0:1, 8:8 + HPC],
                                                         AF.Sqrt), inc=s_s)
                m2v = s_s.n
                sc.wait("vector", s_s, m2v)
                sc.emit("vector", lambda e: e.tensor_scalar(wsca[0:1, 8:8 + HPC], wsca[0:1, 8:8 + HPC],
                                                            -C_SCALE, 0.0, ALU.mult, ALU.add), inc=s_v)
                m3v = s_v.n
                sc.wait("tensor", s_v, m3v)
                sc.wait("tensor", s_v, pnorm_rd)
                def mbc(e):
                    return e.matmul(pO0[:, 0:HPC], ones_row[0:1, 0:128], wsca[0:1, 8:8 + HPC], start=True, stop=True)
                sc.emit("tensor", mbc, inc=s_t)
                mbv = s_t.n
                sc.wait("vector", s_t, mbv)
                sc.emit("vector", lambda e: e.tensor_copy(mtil[:], pO0[:, 0:HPC]), inc=s_v)
                mtv = s_v.n
                sc.wait("tensor", s_v, mtv)

            if debug:
                sc.wait("sync", s_v, s_v.n)
                sc.dma("sync", dbg_mt, mtil[:], "dbgp")
            sc.barrier("pa2")
            if phases <= 3:
                raise _StopBuild

            with ExitStack() as ph:
                wof = [ph.enter_context(nc.sbuf_tensor(f"wof{i}", [128, D], F32)) for i in range(2)]
                wohi = [ph.enter_context(nc.sbuf_tensor(f"wohi{i}", [128, D], BF16)) for i in range(2)]
                wolo = [ph.enter_context(nc.sbuf_tensor(f"wolo{i}", [128, D], BF16)) for i in range(2)]
                t2a = [ph.enter_context(nc.sbuf_tensor(f"t2a{i}", [128, D], F32)) for i in range(2)]
                t2g = [ph.enter_context(nc.sbuf_tensor(f"t2g{i}", [128, D], F32)) for i in range(2)]
                t2b = [ph.enter_context(nc.sbuf_tensor(f"t2b{i}", [128, D], BF16)) for i in range(2)]

                wo_ld = {0: 0, 1: 0}
                wo_st = {0: 0, 1: 0}
                for dc in range(16):
                    bi = dc % 2
                    sc.dwait("sync", f"wost{bi}", wo_st[bi])
                    wo_ld[bi] = sc.dma("sync", wof[bi][:], wo_in[dc * 128:(dc + 1) * 128, :], f"wof{bi}")
                    sc.dwait("vector", f"wof{bi}", wo_ld[bi])
                    sc.emit("vector", lambda e, b=bi: e.tensor_copy(wohi[b][:], wof[b][:]), inc=s_v)
                    hv = s_v.n
                    sc.wait("gpsimd", s_v, hv)
                    sc.emit("gpsimd", lambda e, b=bi: e.tensor_tensor(wolo[b][:], wof[b][:], wohi[b][:],
                                                                      ALU.subtract), inc=s_gp)
                    gv = s_gp.n
                    sc.wait("sync", s_gp, gv)
                    sc.dma("sync", wo_sp[0, dc, :, :], wohi[bi][:], f"wost{bi}")
                    wo_st[bi] = sc.dma("sync", wo_sp[1, dc, :, :], wolo[bi][:], f"wost{bi}")

                t2_ld = {0: 0, 1: 0}
                t2_st = {0: 0, 1: 0}
                for fc in range(64):
                    bi = fc % 2
                    sc.dwait("sync", f"t2st{bi}", t2_st[bi])
                    t2_ld[bi] = sc.dma("sync", wof[bi][:], w2t_in[fc * 128:(fc + 1) * 128, :], f"wof{bi}")
                    sc.dwait("vector", f"wof{bi}", t2_ld[bi])
                    sc.emit("vector", lambda e, b=bi: e.tensor_scalar(
                        t2a[b][:], wof[b][:], wsc[:, 2:3], -1.0, ALU.is_gt, ALU.add), inc=s_v)
                    av = s_v.n
                    sc.dwait("gpsimd", f"wof{bi}", t2_ld[bi])
                    sc.wait("gpsimd", s_v, av)
                    sc.emit("gpsimd", lambda e, b=bi: e.tensor_single_scalar(
                        t2g[b][:], wof[b][:], wsc[:, 3:4], ALU.is_ge), inc=s_gp)
                    gv = s_gp.n
                    sc.wait("vector", s_gp, gv)
                    sc.emit("vector", lambda e, b=bi: e.tensor_tensor(
                        t2b[b][:], t2g[b][:], t2a[b][:], ALU.add), inc=s_v)
                    tv = s_v.n
                    sc.wait("sync", s_v, tv)
                    t2_st[bi] = sc.dma("sync", tern2_sp[fc, :, :], t2b[bi][:], f"t2st{bi}")

            sc.barrier("pw2")
            if phases <= 4:
                raise _StopBuild

            with ExitStack() as ph:
                kh_b = [ph.enter_context(nc.sbuf_tensor(f"kh{i}", [128, T], BF16)) for i in range(2)]
                kl_b = [ph.enter_context(nc.sbuf_tensor(f"kl{i}", [128, T], BF16)) for i in range(2)]
                vh_b = [ph.enter_context(nc.sbuf_tensor(f"vh{i}", [128, NTT, 132], BF16)) for i in range(2)]
                vl_b = [ph.enter_context(nc.sbuf_tensor(f"vl{i}", [128, NTT, 132], BF16)) for i in range(2)]
                q_h = [ph.enter_context(nc.sbuf_tensor(f"q_h{i}", [128, 512], BF16)) for i in range(2)]
                q_l = [ph.enter_context(nc.sbuf_tensor(f"q_l{i}", [128, 512], BF16)) for i in range(2)]
                pf32 = [ph.enter_context(nc.sbuf_tensor(f"pf32_{i}", [128, 512], F32)) for i in range(2)]
                smsk = [ph.enter_context(nc.sbuf_tensor(f"smsk{i}", [128, 512], F32)) for i in range(2)]
                p_hi = [ph.enter_context(nc.sbuf_tensor(f"p_hi{i}", [128, 512], BF16)) for i in range(2)]
                p_lo = [ph.enter_context(nc.sbuf_tensor(f"p_lo{i}", [128, 512], BF16)) for i in range(2)]
                attf = ph.enter_context(nc.sbuf_tensor("attf", [128, 128], F32))
                atth = ph.enter_context(nc.sbuf_tensor("atth", [128, 128], BF16))
                attl = ph.enter_context(nc.sbuf_tensor("attl", [128, 128], BF16))
                recl = ph.enter_context(nc.sbuf_tensor("recl", [128, 1], F32))
                aTh = [ph.enter_context(nc.sbuf_tensor(f"aTh{i}", [128, 512], BF16)) for i in range(2)]
                aTl = [ph.enter_context(nc.sbuf_tensor(f"aTl{i}", [128, 512], BF16)) for i in range(2)]

                pAV = [pA0, pA1, pO0, pO1]
                dbgst = ph.enter_context(nc.sbuf_tensor("dbgst", [128, 516], F32)) if debug else None
                kv_ld = {}
                kv_rd = {0: 0, 1: 0}
                q_rd = {0: 0, 1: 0}
                sc_rd_s = {0: 0, 1: 0}
                sc_rd_v = {0: 0, 1: 0}
                pf_rd_v = {0: 0, 1: 0}
                pf_rd_g = {0: 0, 1: 0}
                smsk_rd = {0: 0, 1: 0}
                p_rd = {0: 0, 1: 0}
                av_rd = {0: 0, 1: 0}
                av_wr = {0: 0, 1: 0}
                aT_st = {0: 0, 1: 0}
                att_tr = 0
                trb_rd = {0: 0, 1: 0}
                qt_store = {}

                def kv_load(qb, h, bi):
                    sc.wait("sync", s_t, kv_rd[bi])
                    sc.dma("sync", kh_b[bi][:], kT_sp[0, h, :, :], f"kv{bi}")
                    sc.dma("sync", kl_b[bi][:], kT_sp[1, h, :, :], f"kv{bi}")
                    sc.dma("sync", vh_b[bi][:], v_sp[0, :, h, :].rearrange("(a p) c -> p a c", p=128),
                           f"kv{bi}")
                    v_ = sc.dma("sync", vl_b[bi][:], v_sp[1, :, h, :].rearrange("(a p) c -> p a c", p=128),
                                f"kv{bi}")
                    kv_ld[(qb, h)] = (bi, v_)

                kv_load(0, 0, 0)
                for qb in range(2):
                    for h in range(HPC):
                        it = qb * HPC + h
                        bi = it % 2
                        kh, kl, vh, vl = kh_b[bi], kl_b[bi], vh_b[bi], vl_b[bi]
                        nqb, nh = (qb, h + 1) if h + 1 < HPC else (qb + 1, 0)
                        if nqb < 2:
                            kv_load(nqb, nh, (it + 1) % 2)
                        sc.dwait("tensor", f"kv{kv_ld[(qb, h)][0]}", kv_ld[(qb, h)][1])

                        for jqt in range(NQT // 2):
                            g = qb * (NQT // 2) + jqt
                            nkc = 4 * (g + 1)
                            qbi = g % 2
                            sc.wait("sync", s_t, q_rd[qbi])
                            r0 = h * (T // 16) + g * 32
                            qsrc0 = qT_sp[0][:, :, r0:r0 + 32].transpose([1, 0, 2])
                            qsrc1 = qT_sp[1][:, :, r0:r0 + 32].transpose([1, 0, 2])
                            sc.dma("sync", q_h[qbi][:].rearrange("p (a b) -> p a b", a=16),
                                   qsrc0, f"qld{qbi}")
                            qldv = sc.dma("sync", q_l[qbi][:].rearrange("p (a b) -> p a b", a=16),
                                          qsrc1, f"qld{qbi}")
                            sc.dwait("tensor", f"qld{qbi}", qldv)
                            for kc in range(nkc):
                                bk = kc % 2
                                pb = pS[bk]
                                sc.wait("tensor", s_s, sc_rd_s[bk])
                                sc.wait("tensor", s_v, sc_rd_v[bk])
                                def smm(e, pb=pb, kc=kc, kh=kh, kl=kl, qh=q_h[qbi], ql=q_l[qbi]):
                                    e.matmul(pb[:], kh[:, kc * 128:(kc + 1) * 128], qh[:], start=True, stop=False)
                                    e.matmul(pb[:], kh[:, kc * 128:(kc + 1) * 128], ql[:], start=False, stop=False)
                                    return e.matmul(pb[:], kl[:, kc * 128:(kc + 1) * 128], qh[:], start=False, stop=True)
                                sc.emit("tensor", smm, inc=s_t)
                                smv = s_t.n
                                q_rd[qbi] = smv
                                boundary = (kc >= 4 * g)
                                if boundary:
                                    mi = kc - 4 * g
                                    sc.wait("vector", s_t, smv)
                                    sc.wait("vector", s_s, smsk_rd[bk])
                                    sc.emit("vector", lambda e, pb=pb, mi=mi, sm=smsk[bk]: e.tensor_tensor(
                                        sm[:], pb[:], masks[:, mi, :], ALU.add), inc=s_v)
                                    sc_rd_v[bk] = s_v.n
                                    sc.wait("scalar", s_v, sc_rd_v[bk])
                                else:
                                    sc.wait("scalar", s_t, smv)
                                pfb = kc % 2
                                sc.wait("scalar", s_v, pf_rd_v[pfb])
                                sc.wait("scalar", s_gp, pf_rd_g[pfb])
                                def expf(e, pb=pb, h=h, bnd=boundary, sm=smsk[bk], pf=pf32[pfb]):
                                    src = sm[:] if bnd else pb[:]
                                    return e.activation(pf[:], src, AF.Exp, bias=mtil[:, h:h + 1], scale=C_SCALE)
                                sc.emit("scalar", expf, inc=s_s)
                                ev_ = s_s.n
                                if debug and qb == 0 and h == 0 and g == 0 and kc == 0:
                                    sc.wait("sync", s_s, ev_)
                                    sc.dma("sync", dbg_p, pf32[pfb][:], "dbgp")
                                    sc.dma("sync", dbg_sm, smsk[bk][:], "dbgp")
                                if boundary:
                                    smsk_rd[bk] = ev_
                                else:
                                    sc_rd_s[bk] = ev_
                                pbuf = kc % 2
                                sc.wait("vector", s_t, p_rd[pbuf])
                                sc.wait("vector", s_s, ev_)
                                sc.emit("vector", lambda e, pf=pf32[pfb], ph_=p_hi[pbuf]: e.tensor_copy(
                                    ph_[:], pf[:]), inc=s_v)
                                phv = s_v.n
                                pf_rd_v[pfb] = phv
                                sc.wait("gpsimd", s_t, p_rd[pbuf])
                                sc.wait("gpsimd", s_v, phv)
                                sc.emit("gpsimd", lambda e, pf=pf32[pfb], ph_=p_hi[pbuf], pl_=p_lo[pbuf]:
                                        e.tensor_tensor(pl_[:], pf[:], ph_[:], ALU.subtract), inc=s_gp)
                                plv = s_gp.n
                                pf_rd_g[pfb] = plv
                                sc.wait("tensor", s_gp, plv)
                                if kc == 0:
                                    sc.wait("tensor", s_v, av_rd[0])
                                    sc.wait("tensor", s_v, av_rd[1])
                                last_kc = (kc == nkc - 1)
                                def avmm(e, kc=kc, ph_=p_hi[pbuf], pl_=p_lo[pbuf], vh=vh, vl=vl,
                                         last=last_kc):
                                    for qs in range(4):
                                        pav = pAV[qs]
                                        e.matmul(pav[:, 0:129], ph_[:, qs * 128:(qs + 1) * 128],
                                                 vh[:, kc, 0:129], start=(kc == 0), stop=False)
                                        e.matmul(pav[:, 0:129], ph_[:, qs * 128:(qs + 1) * 128],
                                                 vl[:, kc, 0:129], start=False, stop=False)
                                        ins = e.matmul(pav[:, 0:129],
                                                       pl_[:, qs * 128:(qs + 1) * 128],
                                                       vh[:, kc, 0:129], start=False,
                                                       stop=last)
                                    return ins
                                sc.emit("tensor", avmm, inc=s_t)
                                avv = s_t.n
                                p_rd[pbuf] = avv
                                av_wr[0] = avv
                                av_wr[1] = avv
                            if debug and qb == 0 and h == 0 and g == 0:
                                sc.wait("vector", s_t, av_wr[0])
                                for _q in range(4):
                                    sc.emit("vector", lambda e, _q=_q: e.tensor_copy(
                                        dbgst[:, _q * 129:(_q + 1) * 129],
                                        pAV[_q][:, 0:129]), inc=s_v)
                                dbgavv = s_v.n
                                sc.wait("sync", s_v, dbgavv)
                                dbgdv = sc.dma("sync", dbg_av[:, 0:516], dbgst[:], "dbgp")
                                sc.dwait("vector", "dbgp", dbgdv)
                            abuf = g % 2
                            sc.dwait("vector", f"agst{abuf}", aT_st[abuf])
                            for qs in range(4):
                                pav = pAV[qs]
                                sl = 0
                                sc.wait("vector", s_t, av_wr[qs // 2])
                                sc.wait("vector", s_t, att_tr)
                                sc.emit("vector", lambda e, pav=pav, sl=sl: e.reciprocal(
                                    recl[:], pav[:, sl + 128:sl + 129]), inc=s_v)
                                sc.emit("vector", lambda e, pav=pav, sl=sl: e.tensor_scalar(
                                    attf[:], pav[:, sl:sl + 128], recl[:], 0.0,
                                    ALU.mult, ALU.add), inc=s_v)
                                sc.emit("vector", lambda e: e.tensor_copy(atth[:], attf[:]), inc=s_v)
                                aev = s_v.n
                                if qs == 3:
                                    av_rd[0] = aev
                                    av_rd[1] = aev
                                sc.wait("gpsimd", s_t, att_tr)
                                sc.wait("gpsimd", s_v, aev)
                                sc.emit("gpsimd", lambda e: e.tensor_tensor(attl[:], attf[:], atth[:],
                                                                            ALU.subtract), inc=s_gp)
                                alv = s_gp.n
                                trb = qs % 2
                                sc.wait("tensor", s_gp, alv)
                                sc.wait("tensor", s_v, trb_rd[trb])
                                def attr(e, ptb=pT[trb]):
                                    e.transpose(ptb[:, 0:128], atth[:], identb[:])
                                    return e.transpose(ptb[:, 128:256], attl[:], identb[:])
                                sc.emit("tensor", attr, inc=s_t)
                                atv = s_t.n
                                att_tr = atv
                                sc.wait("vector", s_t, atv)
                                def attTev(e, ptb=pT[trb], qs=qs, ah=aTh[abuf], al=aTl[abuf]):
                                    e.tensor_copy(ah[:, qs * 128:(qs + 1) * 128], ptb[:, 0:128])
                                    return e.tensor_copy(al[:, qs * 128:(qs + 1) * 128], ptb[:, 128:256])
                                sc.emit("vector", attTev, inc=s_v)
                                trb_rd[trb] = s_v.n
                            agv = s_v.n
                            sc.wait("sync", s_v, agv)
                            sc.dma("sync", ag_in[g, 0, h, :, :], aTh[abuf][:], f"agst{abuf}")
                            aT_st[abuf] = sc.dma("sync", ag_in[g, 1, h, :, :], aTl[abuf][:], f"agst{abuf}")
                            qt_store[g] = (abuf, aT_st[abuf])
                        kv_rd[bi] = s_t.n
                    for jqt in range(NQT // 2):
                        g = qb * (NQT // 2) + jqt
                        sc.dwait("gpsimd", f"agst{qt_store[g][0]}", qt_store[g][1])
                        def ccf(e, g=g):
                            return e.collective_compute(
                                "AllGather", ALU.bypass, replica_groups=RG,
                                ins=[ag_in[g].opt()], outs=[ag_out[g].opt()])
                        sc.emit("gpsimd", ccf, inc=s_cc)

            sc.barrier("pb")
            if phases <= 5:
                raise _StopBuild


            with ExitStack() as ph:
                agb = ph.enter_context(nc.sbuf_tensor("agb", [128, 2, 2, HPC, 512], BF16))
                wos_h = [ph.enter_context(nc.sbuf_tensor(f"wos_h{i}", [128, 16, 512], BF16)) for i in range(2)]
                wos_l = [ph.enter_context(nc.sbuf_tensor(f"wos_l{i}", [128, 16, 512], BF16)) for i in range(2)]
                xm4 = ph.enter_context(nc.sbuf_tensor("xm4", [128, 4, D], F32))

                sc.wait("gpsimd", s_cc, s_cc.n)
                if debug:
                    dbgv = sc.dma("gpsimd", dbg_agin, ag_in, "dbgag")
                    sc.dwait("gpsimd", "dbgag", dbgv)

                ag_rd = 0
                wos_rd = {0: 0, 1: 0}
                po_rd = {0: 0, 1: 0}
                xm_st = 0
                xm_rd_v = 0
                xm_rd_s = 0

                sem_agb = sc.sem("dma_agb")
                sem_xm4 = sc.sem("dma_xm4ld")
                sem_xmsp = sc.sem("dma_xmsp")
                for j in range(NOJ):
                    sc.wait("gpsimd", s_t, ag_rd)
                    def agld(e, j=j):
                        src = ag_out[j].transpose([3, 0, 1, 2, 4])
                        src2 = bass.AP(src.tensor, src.offset + dyn["ag"], src.ap)
                        return e.dma_start(out=agb[:], in_=src2).then_inc(sem_agb.h, 16)
                    sem_agb.inc(16)
                    sc.ops["gpsimd"].append(lambda e, f=agld: f(e))
                    agldv = sem_agb.n
                    sc.wait("gpsimd", sem_xmsp, xm_st)
                    sc.wait("gpsimd", s_v, xm_rd_v)
                    sc.wait("gpsimd", s_s, xm_rd_s)
                    for qs in range(4):
                        tt = j * 4 + qs
                        def xld(e, tt=tt, qs=qs):
                            return e.dma_start(out=xm4[:, qs, :],
                                               in_=xq_in[tt * 128:(tt + 1) * 128, :]).then_inc(
                                sem_xm4.h, 16)
                        sem_xm4.inc(16)
                        sc.ops["gpsimd"].append(lambda e, f=xld: f(e))
                    xm_ld = sem_xm4.n
                    sc.wait("tensor", sem_agb, agldv)
                    for fc in range(4):
                        wbi = fc % 2
                        sc.wait("sync", s_t, wos_rd[wbi])
                        sc.dma("sync", wos_h[wbi][:],
                               wo_sp[0, :, :, fc * 512:(fc + 1) * 512].transpose([1, 0, 2]), f"wos{wbi}")
                        wldv = sc.dma("sync", wos_l[wbi][:],
                                      wo_sp[1, :, :, fc * 512:(fc + 1) * 512].transpose([1, 0, 2]),
                                      f"wos{wbi}")
                        sc.dwait("tensor", f"wos{wbi}", wldv)
                        for qs in range(4):
                            bk = qs % 2
                            pb = pO[bk]
                            sc.wait("tensor", s_v, po_rd[bk])
                            def ojmm(e, qs=qs, pb=pb, wbi=wbi, fc=fc):
                                first = True
                                for rk in range(2):
                                    for hd in range(HPC):
                                        dcp = rk * HPC + hd
                                        hi_ap = agb[:, rk, 0, hd, qs * 128:(qs + 1) * 128]
                                        lo_ap = agb[:, rk, 1, hd, qs * 128:(qs + 1) * 128]
                                        e.matmul(pb[:], hi_ap, wos_h[wbi][:, dcp, :], start=first, stop=False)
                                        first = False
                                        e.matmul(pb[:], hi_ap, wos_l[wbi][:, dcp, :], start=False, stop=False)
                                        e.matmul(pb[:], lo_ap, wos_h[wbi][:, dcp, :], start=False, stop=False)
                                return e.matmul(pb[:], ones_row[0:1, 0:128], bo_sb[0:1, fc * 512:(fc + 1) * 512],
                                         start=False, stop=True)
                            sc.emit("tensor", ojmm, inc=s_t)
                            mmv = s_t.n
                            ag_rd = mmv
                            wos_rd[wbi] = mmv
                            sc.wait("vector", s_t, mmv)
                            sc.wait("vector", sem_xm4, xm_ld)
                            sc.emit("vector", lambda e, pb=pb, qs=qs, fc=fc: e.tensor_tensor(
                                xm4[:, qs, fc * 512:(fc + 1) * 512], pb[:],
                                xm4[:, qs, fc * 512:(fc + 1) * 512], ALU.add), inc=s_v)
                            po_rd[bk] = s_v.n
                    evv = s_v.n
                    sc.wait("vector", s_v, evv)
                    sc.wait("scalar", s_v, evv)
                    for qs in range(4):
                        tt = j * 4 + qs
                        def ln2acc_v(e, qs=qs, tt=tt):
                            return e.reduce_sum(ln2s[:, tt:tt + 1], xm4[:, qs, :], AX.X)
                        sc.emit("vector", ln2acc_v, inc=s_v)
                        sc.wait("scalar", s_s, 0)
                        def ln2acc_s(e, qs=qs):
                            return e.activation(sq_scr[:], xm4[:, qs, :], AF.Square, accum_out=stat_sq[:])
                        sc.emit("scalar", ln2acc_s, inc=s_s)
                        sqv = s_s.n
                        sc.wait("vector", s_s, sqv)
                        sc.emit("vector", lambda e, tt=tt: e.tensor_copy(ln2s[:, NOT + tt:NOT + tt + 1],
                                                                         stat_sq[:]), inc=s_v)
                        stv = s_v.n
                        sc.wait("scalar", s_v, stv)
                        sc.wait("sync", s_v, stv)
                        sc.dma("sync", xmid_sp[tt * 128:(tt + 1) * 128, :], xm4[:, qs, :], sem_xmsp)
                    xm_st = sem_xmsp.n
                    xm_rd_v = s_v.n
                    xm_rd_s = s_s.n
                    sc.wait("sync", s_v, xm_rd_v)

            sc.barrier("poj")
            if phases <= 6:
                raise _StopBuild

            phX = ExitStack()
            xiT = phX.enter_context(nc.sbuf_tensor("xiT", [128, 16, OWN], BF16))
            with ExitStack() as ph:
                xm_l = [ph.enter_context(nc.sbuf_tensor(f"xm_l{i}", [128, D], F32)) for i in range(2)]
                xh2 = ph.enter_context(nc.sbuf_tensor("xh2", [128, D], F32))
                qt_i = ph.enter_context(nc.sbuf_tensor("qt_i", [128, D], I32))
                qt_y = ph.enter_context(nc.sbuf_tensor("qt_y", [128, D], F32))
                qt_d = ph.enter_context(nc.sbuf_tensor("qt_d", [128, D], F32))
                qt_c = ph.enter_context(nc.sbuf_tensor("qt_c", [128, D], F32))
                qt_r = ph.enter_context(nc.sbuf_tensor("qt_r", [128, D], F32))
                x_int = ph.enter_context(nc.sbuf_tensor("x_int", [128, D], BF16))

                for tt in range(NOT):
                    def ln2f_v(e, tt=tt):
                        e.tensor_scalar(tiny[:, 0:1], ln2s[:, tt:tt + 1], 1.0 / D, 0.0, ALU.mult, ALU.add)
                        return e.tensor_scalar(tiny[:, 1:2], ln2s[:, NOT + tt:NOT + tt + 1], 1.0 / D, 0.0,
                                               ALU.mult, ALU.add)
                    sc.emit("vector", ln2f_v, inc=s_v)
                    sc.emit("vector", lambda e: e.tensor_tensor(tiny[:, 2:3], tiny[:, 0:1],
                                                                tiny[:, 0:1], ALU.mult), inc=s_v)
                    sc.emit("vector", lambda e: e.tensor_tensor(tiny[:, 3:4], tiny[:, 1:2],
                                                                tiny[:, 2:3], ALU.subtract), inc=s_v)
                    vv = s_v.n
                    sc.wait("scalar", s_v, vv)
                    sc.emit("scalar", lambda e: e.activation(tiny[:, 4:5], tiny[:, 3:4], AF.Sqrt,
                                                             bias=epsc[:]), inc=s_s)
                    sv = s_s.n
                    sc.wait("vector", s_s, sv)
                    sc.emit("vector", lambda e, tt=tt: e.reciprocal(
                        ln2s[:, 2 * NOT + tt:2 * NOT + tt + 1], tiny[:, 4:5]), inc=s_v)
                    sc.emit("vector", lambda e, tt=tt: e.scalar_tensor_tensor(
                        ln2s[:, 3 * NOT + tt:3 * NOT + tt + 1], tiny[:, 0:1], -1.0,
                        ln2s[:, 2 * NOT + tt:2 * NOT + tt + 1], ALU.mult, ALU.mult), inc=s_v)
                ln2fin = s_v.n

                xm_rd = {0: 0, 1: 0}
                trb_rd2 = {0: 0, 1: 0}
                xh2_rd_v = 0
                xh2_rd_g = 0
                for tt in range(NOT):
                    bi = tt % 2
                    sc.wait("sync", s_s, xm_rd[bi])
                    xldv = sc.dma("sync", xm_l[bi][:], xmid_sp[tt * 128:(tt + 1) * 128, :], f"xml{bi}")
                    sc.dwait("scalar", f"xml{bi}", xldv)
                    sc.wait("scalar", s_v, ln2fin)
                    sc.wait("scalar", s_v, xh2_rd_v)
                    sc.wait("scalar", s_gp, xh2_rd_g)
                    sc.emit("scalar", lambda e, b=bi, tt=tt: e.activation(
                        xh2[:], xm_l[b][:], AF.Identity, bias=ln2s[:, 3 * NOT + tt:3 * NOT + tt + 1],
                        scale=ln2s[:, 2 * NOT + tt:2 * NOT + tt + 1]), inc=s_s)
                    xm_rd[bi] = s_s.n
                    xhv = s_s.n
                    sc.wait("vector", s_s, xhv)
                    sc.emit("vector", lambda e: e.tensor_reduce(tiny[:, 0:1], xh2[:], AX.X,
                                                                ALU.max, True), inc=s_v)
                    sc.emit("vector", lambda e: e.tensor_scalar(tiny[:, 0:1], tiny[:, 0:1], EPS,
                                                                None, ALU.max), inc=s_v)
                    sc.emit("vector", lambda e: e.reciprocal(tiny[:, 1:2], tiny[:, 0:1]), inc=s_v)
                    sc.emit("vector", lambda e, tt=tt: e.tensor_scalar(
                        qs_sb[:, tt:tt + 1], tiny[:, 1:2], 127.0, 0.0, ALU.mult, ALU.add), inc=s_v)
                    sc.emit("vector", lambda e: e.tensor_scalar(
                        tiny[:, 2:3], tiny[:, 0:1], 1.0 / 127.0, 0.0, ALU.mult, ALU.add), inc=s_v)
                    sc.emit("vector", lambda e, tt=tt: e.tensor_tensor(
                        qs_sb[:, NOT + tt:NOT + tt + 1], tiny[:, 2:3], wsc[:, 4:5], ALU.mult), inc=s_v)
                    qsv = s_v.n
                    xh2_rd_v = qsv
                    sc.wait("gpsimd", s_v, qsv)
                    sc.wait("gpsimd", s_s, xhv)
                    sc.emit("gpsimd", lambda e, tt=tt: e.tensor_scalar(
                        qt_y[:], xh2[:], qs_sb[:, tt:tt + 1], 0.0, ALU.mult, ALU.add), inc=s_gp)
                    yv = s_gp.n
                    xh2_rd_g = yv
                    sc.wait("vector", s_gp, yv)
                    sc.emit("vector", lambda e: e.tensor_copy(qt_i[:], qt_y[:]), inc=s_v)
                    i1v = s_v.n
                    sc.emit("vector", lambda e: e.tensor_tensor(qt_d[:], qt_y[:], qt_i[:],
                                                                ALU.subtract), inc=s_v)
                    dv = s_v.n
                    sc.wait("gpsimd", s_v, dv)
                    sc.emit("gpsimd", lambda e: e.tensor_scalar(qt_c[:], qt_d[:], 0.5, -1.0,
                                                                ALU.is_gt, ALU.add), inc=s_gp)
                    sc.emit("gpsimd", lambda e: e.tensor_single_scalar(qt_r[:], qt_d[:], -0.5,
                                                                       ALU.is_ge), inc=s_gp)
                    sc.emit("gpsimd", lambda e: e.tensor_tensor(qt_c[:], qt_r[:], qt_c[:],
                                                                ALU.add), inc=s_gp)
                    corv = s_gp.n
                    sc.wait("vector", s_gp, corv)
                    sc.emit("vector", lambda e: e.tensor_tensor(qt_d[:], qt_i[:], qt_c[:], ALU.add), inc=s_v)
                    resv = s_v.n
                    sc.emit("vector", lambda e: e.tensor_scalar(x_int[:], qt_d[:], 127.0, -128.0,
                                                                ALU.min, ALU.max), inc=s_v)
                    xiv = s_v.n
                    if debug:
                        sc.wait("sync", s_v, xiv)
                        sc.dma("sync", dbg_xint[tt * 128:(tt + 1) * 128, :], x_int[:], "dbgxi")
                    sc.wait("tensor", s_v, xiv)
                    for dc in range(16):
                        trb = dc % 2
                        sc.wait("tensor", s_v, trb_rd2[trb])
                        def xtr(e, ptb=pT[trb], dc=dc):
                            return e.transpose(ptb[:, 0:128], x_int[:, dc * 128:(dc + 1) * 128], identb[:])
                        sc.emit("tensor", xtr, inc=s_t)
                        tv = s_t.n
                        sc.wait("vector", s_t, tv)
                        sc.emit("vector", lambda e, ptb=pT[trb], dc=dc, tt=tt: e.tensor_copy(
                            xiT[:, dc, tt * 128:(tt + 1) * 128], ptb[:, 0:128]), inc=s_v)
                        trb_rd2[trb] = s_v.n
                xiT_done = s_v.n

            sc.barrier("pd0")
            if phases <= 7:
                raise _StopBuild

            with ExitStack() as ph:
                w1s = [ph.enter_context(nc.sbuf_tensor(f"w1s{i}", [128, 2, 1024], F32)) for i in range(2)]
                t1a = ph.enter_context(nc.sbuf_tensor("t1a", [128, 2048], F32))
                t1g = ph.enter_context(nc.sbuf_tensor("t1g", [128, 2048], F32))
                tern1 = [ph.enter_context(nc.sbuf_tensor(f"tern1_{i}", [128, 16, 1024], BF16)) for i in range(2)]
                hev = [ph.enter_context(nc.sbuf_tensor(f"hev{i}", [128, 512], F32)) for i in range(2)]

                sc.wait("tensor", s_v, xiT_done)

                w1_rd = {0: 0, 1: 0}
                w1v_rd = {0: 0, 1: 0}
                t1_rd = {0: 0, 1: 0}
                t1a_rd = 0
                t1g_rd = 0
                hev_st = {0: 0, 1: 0}
                habs_rd = {0: 0, 1: 0}
                pd1_rd = {0: 0, 1: 0, 2: 0, 3: 0}
                pD = [pO0, pO1, pS0, pS1]

                for fg in range(8):
                    tbi = fg % 2
                    sc.wait("gpsimd", s_t, t1_rd[tbi])
                    for ch in range(8):
                        wbi = ch % 2
                        sc.wait("sync", s_gp, w1_rd[wbi])
                        sc.wait("sync", s_v, w1v_rd[wbi])
                        ldv = sc.dma("sync", w1s[wbi][:],
                                     w1t_in[ch * 256:(ch + 1) * 256, fg * 1024:(fg + 1) * 1024]
                                     .rearrange("(a p) f -> p a f", p=128), f"w1s{wbi}")
                        sc.dwait("vector", f"w1s{wbi}", ldv)
                        sc.wait("vector", s_v, t1a_rd)
                        sc.emit("vector", lambda e, b=wbi: e.tensor_scalar(
                            t1a[:], w1s[b][:].rearrange("p a b -> p (a b)"), wsc[:, 0:1], -1.0,
                            ALU.is_gt, ALU.add), inc=s_v)
                        av = s_v.n
                        sc.wait("gpsimd", s_v, t1g_rd)
                        sc.dwait("gpsimd", f"w1s{wbi}", ldv)
                        sc.emit("gpsimd", lambda e, b=wbi: e.tensor_single_scalar(
                            t1g[:], w1s[b][:].rearrange("p a b -> p (a b)"), wsc[:, 1:2],
                            ALU.is_ge), inc=s_gp)
                        gv = s_gp.n
                        w1_rd[wbi] = gv
                        sc.wait("vector", s_gp, gv)
                        sc.wait("vector", s_v, av)
                        def ternf(e, ch=ch, tbi=tbi):
                            dst = tern1[tbi][:, 2 * ch:2 * ch + 2, :].rearrange("p a b -> p (a b)")
                            return e.tensor_tensor(dst, t1g[:], t1a[:], ALU.add)
                        sc.emit("vector", ternf, inc=s_v)
                        t1a_rd = s_v.n
                        t1g_rd = s_v.n
                        w1v_rd[wbi] = av
                    t1v = s_v.n
                    sc.wait("tensor", s_v, t1v)
                    for tt in range(NOT):
                        for fi in range(2):
                            b = (tt * 2 + fi) % 4
                            pb = pD[b]
                            sc.wait("tensor", s_s, pd1_rd[b])
                            def mm1(e, pb=pb, tt=tt, fi=fi, tbi=tbi):
                                for dc in range(16):
                                    ins = e.matmul(pb[:], xiT[:, dc, tt * 128:(tt + 1) * 128],
                                                   tern1[tbi][:, dc, fi * 512:(fi + 1) * 512],
                                                   start=(dc == 0), stop=(dc == 15))
                                return ins
                            sc.emit("tensor", mm1, inc=s_t)
                            mmv = s_t.n
                            t1_rd[tbi] = mmv
                            hbi = fi
                            sc.wait("scalar", s_t, mmv)
                            sc.dwait("scalar", f"hsp{hbi}", hev_st[hbi])
                            sc.wait("scalar", s_v, habs_rd[hbi])
                            def gel(e, pb=pb, tt=tt, b=hbi):
                                return e.activation(hev[b][:], pb[:], AF.Gelu,
                                             scale=qs_sb[:, NOT + tt:NOT + tt + 1])
                            sc.emit("scalar", gel, inc=s_s)
                            pd1_rd[b] = s_s.n
                            gv = s_s.n
                            sc.wait("vector", s_s, gv)
                            sc.emit("vector", lambda e, b=hbi: e.tensor_reduce(
                                tiny[:, 0:1], hev[b][:], AX.X, ALU.max, True), inc=s_v)
                            sc.emit("vector", lambda e, tt=tt: e.tensor_tensor(
                                hs_sb[:, tt:tt + 1], hs_sb[:, tt:tt + 1], tiny[:, 0:1],
                                ALU.max), inc=s_v)
                            habs_rd[hbi] = s_v.n
                            hv = s_v.n
                            sc.wait("sync", s_v, hv)
                            hev_st[hbi] = sc.dma("sync",
                                                 h_sp[tt * 128:(tt + 1) * 128,
                                                      fg * 1024 + fi * 512:fg * 1024 + (fi + 1) * 512],
                                                 hev[hbi][:], f"hsp{hbi}")

            phX.close()
            sc.barrier("pd1")
            if phases <= 8:
                raise _StopBuild

            with ExitStack() as ph:
                hT = ph.enter_context(nc.sbuf_tensor("hT", [128, 64, 512], BF16))
                hld = [ph.enter_context(nc.sbuf_tensor(f"hld{i}", [128, DFF // 2], F32)) for i in range(2)]
                q2_i = ph.enter_context(nc.sbuf_tensor("q2_i", [128, 2048], I32))
                q2_y = ph.enter_context(nc.sbuf_tensor("q2_y", [128, 2048], F32))
                q2_d = ph.enter_context(nc.sbuf_tensor("q2_d", [128, 2048], F32))
                q2_c = ph.enter_context(nc.sbuf_tensor("q2_c", [128, 2048], F32))
                h_int = ph.enter_context(nc.sbuf_tensor("h_int", [128, 2048], BF16))
                t2s = [ph.enter_context(nc.sbuf_tensor(f"t2s{i}", [128, 16, 512], BF16)) for i in range(2)]
                yst = [ph.enter_context(nc.sbuf_tensor(f"yst{i}", [128, 512], F32)) for i in range(2)]
                xms = [ph.enter_context(nc.sbuf_tensor(f"xms{i}", [128, 512], F32)) for i in range(2)]

                for tt in range(NOT):
                    sc.emit("vector", lambda e, tt=tt: e.tensor_scalar(
                        tiny[:, 0:1], hs_sb[:, tt:tt + 1], EPS, None, ALU.max), inc=s_v)
                    sc.emit("vector", lambda e: e.reciprocal(tiny[:, 1:2], tiny[:, 0:1]), inc=s_v)
                    sc.emit("vector", lambda e, tt=tt: e.tensor_scalar(
                        hs_sb[:, NOT + tt:NOT + tt + 1], tiny[:, 1:2], 127.0, 0.0,
                        ALU.mult, ALU.add), inc=s_v)
                    sc.emit("vector", lambda e: e.tensor_scalar(
                        tiny[:, 2:3], tiny[:, 0:1], 1.0 / 127.0, 0.0, ALU.mult, ALU.add), inc=s_v)
                    sc.emit("vector", lambda e, tt=tt: e.tensor_tensor(
                        hs_sb[:, 2 * NOT + tt:2 * NOT + tt + 1], tiny[:, 2:3],
                        wsc[:, 5:6], ALU.mult), inc=s_v)
                hscv = s_v.n
                sc.wait("scalar", s_v, hscv)
                sc.wait("gpsimd", s_v, hscv)

                h_rd = {0: 0, 1: 0}
                trb_rd3 = {0: 0, 1: 0}
                hT_rd = 0
                t2_rd = {0: 0, 1: 0}
                po2_rd = {0: 0, 1: 0, 2: 0, 3: 0}
                mm2_done = {0: 0, 1: 0, 2: 0, 3: 0}
                xms_rd = {0: 0, 1: 0}
                yst_rd = {0: 0, 1: 0}
                pD = [pO0, pO1, pS0, pS1]

                for tc in range(NOT // 4):
                    sc.wait("tensor", s_t, hT_rd)
                    for tloc in range(4):
                        tt = tc * 4 + tloc
                        for half in range(2):
                            hbi = half
                            sc.wait("sync", s_gp, h_rd[hbi])
                            hldv = sc.dma("sync", hld[hbi][:],
                                          h_sp[tt * 128:(tt + 1) * 128,
                                               half * (DFF // 2):(half + 1) * (DFF // 2)], f"hld{hbi}")
                            for ch in range(2):
                                cs = half * (DFF // 2) + ch * 2048
                                src = hld[hbi][:, ch * 2048:(ch + 1) * 2048]
                                sc.dwait("gpsimd", f"hld{hbi}", hldv)
                                sc.emit("gpsimd", lambda e, src=src, tt=tt: e.tensor_scalar(
                                    q2_y[:], src, hs_sb[:, NOT + tt:NOT + tt + 1], 0.0,
                                    ALU.mult, ALU.add), inc=s_gp)
                                yv = s_gp.n
                                h_rd[hbi] = yv
                                sc.wait("vector", s_gp, yv)
                                sc.emit("vector", lambda e: e.tensor_copy(q2_i[:], q2_y[:]), inc=s_v)
                                sc.emit("vector", lambda e: e.tensor_tensor(q2_d[:], q2_y[:], q2_i[:],
                                                                            ALU.subtract), inc=s_v)
                                dv = s_v.n
                                sc.wait("gpsimd", s_v, dv)
                                sc.emit("gpsimd", lambda e: e.tensor_scalar(q2_c[:], q2_d[:], 0.5, -1.0,
                                                                            ALU.is_gt, ALU.add), inc=s_gp)
                                sc.emit("gpsimd", lambda e: e.tensor_single_scalar(
                                    q2_y[:], q2_d[:], -0.5, ALU.is_ge), inc=s_gp)
                                sc.emit("gpsimd", lambda e: e.tensor_tensor(q2_c[:], q2_y[:], q2_c[:],
                                                                            ALU.add), inc=s_gp)
                                corv = s_gp.n
                                sc.wait("vector", s_gp, corv)
                                sc.emit("vector", lambda e: e.tensor_tensor(q2_d[:], q2_i[:], q2_c[:],
                                                                            ALU.add), inc=s_v)
                                sc.emit("vector", lambda e: e.tensor_scalar(h_int[:], q2_d[:], 127.0,
                                                                            -128.0, ALU.min, ALU.max),
                                        inc=s_v)
                                hiv = s_v.n
                                sc.wait("tensor", s_v, hiv)
                                for dcl in range(16):
                                    fc = cs // 128 + dcl
                                    trb = dcl % 2
                                    sc.wait("tensor", s_v, trb_rd3[trb])
                                    def htr(e, ptb=pT[trb], dcl=dcl):
                                        return e.transpose(ptb[:, 0:128], h_int[:, dcl * 128:(dcl + 1) * 128],
                                                    identb[:])
                                    sc.emit("tensor", htr, inc=s_t)
                                    tv = s_t.n
                                    sc.wait("vector", s_t, tv)
                                    sc.emit("vector", lambda e, ptb=pT[trb], fc=fc, tloc=tloc:
                                            e.tensor_copy(hT[:, fc, tloc * 128:(tloc + 1) * 128],
                                                          ptb[:, 0:128]), inc=s_v)
                                    trb_rd3[trb] = s_v.n
                    hTv = s_v.n
                    sc.wait("tensor", s_v, hTv)
                    for ec in range(4):
                        for fcg in range(4):
                            t2bi = fcg % 2
                            sc.wait("sync", s_t, t2_rd[t2bi])
                            t2ldv = sc.dma("sync", t2s[t2bi][:],
                                           tern2_sp[fcg * 16:(fcg + 1) * 16, :, ec * 512:(ec + 1) * 512]
                                           .transpose([1, 0, 2]), f"t2s{t2bi}")
                            sc.dwait("tensor", f"t2s{t2bi}", t2ldv)
                            for tloc in range(4):
                                b = tloc
                                pb = pD[b]
                                if fcg == 0:
                                    sc.wait("tensor", s_v, po2_rd[b])
                                def mm2(e, pb=pb, tloc=tloc, fcg=fcg, t2bi=t2bi):
                                    for fl in range(16):
                                        fc = fcg * 16 + fl
                                        ins = e.matmul(pb[:], hT[:, fc, tloc * 128:(tloc + 1) * 128],
                                                       t2s[t2bi][:, fl, :],
                                                       start=(fcg == 0 and fl == 0),
                                                       stop=(fcg == 3 and fl == 15))
                                    return ins
                                sc.emit("tensor", mm2, inc=s_t)
                                t2_rd[t2bi] = s_t.n
                                hT_rd = s_t.n
                                mm2_done[b] = s_t.n
                        for tloc in range(4):
                            tt = tc * 4 + tloc
                            b = tloc
                            pb = pD[b]
                            xbi = tloc % 2
                            ybi = tloc % 2
                            sc.wait("sync", s_v, xms_rd[xbi])
                            xmldv = sc.dma("sync", xms[xbi][:],
                                           xmid_sp[tt * 128:(tt + 1) * 128, ec * 512:(ec + 1) * 512],
                                           f"xms{xbi}")
                            sc.wait("vector", s_t, mm2_done[b])
                            sc.dwait("vector", f"xms{xbi}", xmldv)
                            sc.dwait("vector", f"yst{ybi}", yst_rd[ybi])
                            def yev(e, pb=pb, tt=tt, xbi=xbi, ybi=ybi):
                                return e.scalar_tensor_tensor(yst[ybi][:], pb[:],
                                                       hs_sb[:, 2 * NOT + tt:2 * NOT + tt + 1],
                                                       xms[xbi][:], ALU.mult, ALU.add)
                            sc.emit("vector", yev, inc=s_v)
                            po2_rd[b] = s_v.n
                            xms_rd[xbi] = s_v.n
                            yv = s_v.n
                            sc.wait("sync", s_v, yv)
                            yst_rd[ybi] = sc.dma("sync",
                                                 y_out[tt * 128:(tt + 1) * 128, ec * 512:(ec + 1) * 512],
                                                 yst[ybi][:], f"yst{ybi}")

            sc.dwait("sync", "yst0", sc.sem("dma_yst0").n)
            sc.dwait("sync", "yst1", sc.sem("dma_yst1").n)
            sc.dwait("gpsimd", "yst0", sc.sem("dma_yst0").n)
            sc.dwait("gpsimd", "yst1", sc.sem("dma_yst1").n)


        except _StopBuild:
            pass

        with nc.Block() as block:
            def mk(name):
                def run(e):
                    for op in sc.ops[name]:
                        op(e)
                return run
            block.sync(mk("sync"))
            block.tensor(mk("tensor"))
            block.vector(mk("vector"))
            block.scalar(mk("scalar"))
            block.gpsimd(mk("gpsimd"))

    return nc


def _build_masks():
    j = np.arange(512)
    tloc = (j % 32) * 16 + j // 32
    k = np.arange(128)[:, None]
    out = np.zeros((4, 128, 512), dtype=np.float32)
    for i in range(4):
        out[i] = np.where(k + 128 * i <= tloc[None, :], 0.0, NEG).astype(np.float32)
    return out


_NC_CACHE = {}


def _get_nc(T, debug=False, phases=99):
    key = (T, debug, phases)
    if key not in _NC_CACHE:
        _NC_CACHE[key] = build_program(T=T, debug=debug, phases=phases)
    return _NC_CACHE[key]


def kernel(L0_attn_x, L0_attn_Wq, L0_attn_bq, L0_attn_Wk, L0_attn_bk,
           L0_attn_Wv, L0_attn_bv, L0_attn_Wo, L0_attn_bo, L0_ffn_w1, L0_ffn_w2,
           _T=None, _debug=False, _phases=99):
    import ml_dtypes
    x = np.ascontiguousarray(np.asarray(L0_attn_x, dtype=np.float32))
    Bsz, Tfull, Dm = x.shape
    T = _T or Tfull
    wq = np.asarray(L0_attn_Wq, dtype=np.float32)
    wk = np.asarray(L0_attn_Wk, dtype=np.float32)
    wv = np.asarray(L0_attn_Wv, dtype=np.float32)
    wo = np.ascontiguousarray(np.asarray(L0_attn_Wo, dtype=np.float32))
    bq = np.asarray(L0_attn_bq, dtype=np.float32)
    bk = np.asarray(L0_attn_bk, dtype=np.float32)
    bv = np.asarray(L0_attn_bv, dtype=np.float32)
    bo = np.asarray(L0_attn_bo, dtype=np.float32)
    w1t = np.ascontiguousarray(np.asarray(L0_ffn_w1, dtype=np.float32).T)
    w2t = np.ascontiguousarray(np.asarray(L0_ffn_w2, dtype=np.float32).T)
    masks = _build_masks()
    identb = np.eye(128, dtype=np.float32).astype(ml_dtypes.bfloat16)

    nc = _get_nc(T, _debug, _phases)
    in_maps = []
    OWN_ = T // 2
    l = np.arange(OWN_)
    for c in range(8):
        p, half = c // 2, c % 2
        g_l = half * (T // 1024) + l // 512
        jj = l % 512
        tprime = g_l * 512 + (jj % 32) * 16 + jj // 32
        hsl = slice(half * HPC * DH, (half + 1) * HPC * DH)
        im = {
            "x": np.ascontiguousarray(x[p, :T, :]),
            "xq": np.ascontiguousarray(x[p, tprime, :]),
            "wq": wq,
            "wk": np.ascontiguousarray(wk[:, hsl]),
            "wv": np.ascontiguousarray(wv[:, hsl]),
            "bq": np.ascontiguousarray(bq.reshape(16, 128).T),
            "bk": np.ascontiguousarray(bk[hsl].reshape(HPC, 128).T),
            "bv": np.ascontiguousarray(bv[hsl].reshape(1, HPC * DH)),
            "wo": wo,
            "bo": np.ascontiguousarray(bo.reshape(1, D)),
            "w1t": w1t,
            "w2t": w2t,
            "masks": masks,
            "identb": identb,
        }
        in_maps.append(im)

    res = run_bass_kernel_spmd(nc, in_maps, core_ids=list(range(8)))
    out = np.empty((Bsz, T, Dm), dtype=np.float32)
    for c in range(8):
        p, half = c // 2, c % 2
        g_l = half * (T // 1024) + l // 512
        jj = l % 512
        tprime = g_l * 512 + (jj % 32) * 16 + jj // 32
        out[p, tprime, :] = np.asarray(res.results[c]["y"]).reshape(T // 2, D)
    if _debug:
        return out, res
    if T != Tfull:
        full = np.zeros((Bsz, Tfull, Dm), dtype=np.float32)
        full[:, :T, :] = out
        return full
    return out


# version 04 (final)
# speedup vs baseline: 1.0275x; 1.0275x over previous
import numpy as np
import concourse.bass as bass
import concourse.mybir as mybir
from concourse.bass_utils import run_bass_kernel_spmd
from contextlib import ExitStack

F32 = mybir.dt.float32
BF16 = mybir.dt.bfloat16
I32 = mybir.dt.int32
AF = mybir.ActivationFunctionType
ALU = mybir.AluOpType
AX = mybir.AxisListType

D = 2048
DFF = 8192
NH, DH, HPC = 16, 128, 8
EPS = 1e-5
NEG = -1.0e30
C_SCALE = float(1.0 / np.sqrt(DH).astype(np.float32))

ENGS = ("sync", "tensor", "vector", "scalar", "gpsimd")
import os as _os
SELFWAIT = _os.environ.get("K_SELFWAIT") == "1"
RG = [[0, 1], [2, 3], [4, 5], [6, 7]]


class Sem:
    def __init__(self, handle, name):
        self.h, self.name, self.n = handle, name, 0

    def inc(self, by=1):
        self.n += by
        return self.n


class Sched:

    def __init__(self, nc, stack):
        self.nc = nc
        self.ops = {e: [] for e in ENGS}
        self.last_wait = {e: {} for e in ENGS}
        self._stack = stack
        self._sems = {}

    def sem(self, name):
        if name not in self._sems:
            h = self._stack.enter_context(self.nc.semaphore(name))
            self._sems[name] = Sem(h, name)
        return self._sems[name]

    def wait(self, eng, sem, val):
        if val <= 0:
            return
        lw = self.last_wait[eng]
        if lw.get(sem.name, -1) >= val:
            return
        lw[sem.name] = val
        self.ops[eng].append(lambda e, h=sem.h, v=val: e.wait_ge(h, v))

    def emit(self, eng, fn, inc=None, inc_by=1):
        if inc is not None:
            if eng in ("vector", "scalar", "gpsimd"):
                self.wait(eng, inc, inc.n)
            v = inc.inc(inc_by)
            def wrapped(e, fn=fn, h=inc.h, by=inc_by):
                ins = fn(e)
                assert ins is not None, "emitter must return its last instruction"
                ins.then_inc(h, by)
            self.ops[eng].append(wrapped)
            return v
        self.ops[eng].append(fn)
        return None

    def dma(self, eng, out, in_, inc):
        if isinstance(inc, str):
            inc = self.sem("dma_" + inc)
        v = inc.inc(16)
        def fn(e, out=out, in_=in_, h=inc.h):
            e.dma_start(out=out, in_=in_).then_inc(h, 16)
        self.ops[eng].append(fn)
        return v

    def dwait(self, eng, key, val):
        self.wait(eng, self.sem("dma_" + key), val)

    def barrier(self, name):
        snap = [(s, s.n) for s in self._sems.values() if s.n > 0]
        for e in ENGS:
            for s, v in snap:
                self.wait(e, s, v)


class _StopBuild(Exception):
    pass


def build_program(T=4096, debug=False, phases=99):
    import os
    EN_TR = os.environ.get("K_NO_TR") != "1"
    EN_WS = os.environ.get("K_NO_WS") != "1"
    EN_SP = os.environ.get("K_NO_SP") != "1"
    K_NTS = int(os.environ.get("K_NTS", "99"))
    NTT = T // 128
    NTS = T // 512
    NQT = NTS
    OWN = T // 2
    NOT = OWN // 128
    NOJ = OWN // 512

    nc = bass.Bass("TRN2", target_bir_lowering=False,
                   detect_race_conditions=_os.environ.get("K_RACEDET") == "1")

    ikind = "ExternalOutput" if debug else "Internal"

    x_in = nc.dram_tensor("x", [T, D], F32, kind="ExternalInput").ap()
    wq_in = nc.dram_tensor("wq", [D, D], F32, kind="ExternalInput").ap()
    xq_in = nc.dram_tensor("xq", [OWN, D], F32, kind="ExternalInput").ap()
    wk_in = nc.dram_tensor("wk", [D, HPC * DH], F32, kind="ExternalInput").ap()
    wv_in = nc.dram_tensor("wv", [D, HPC * DH], F32, kind="ExternalInput").ap()
    bq_in = nc.dram_tensor("bq", [128, 16], F32, kind="ExternalInput").ap()
    bk_in = nc.dram_tensor("bk", [128, HPC], F32, kind="ExternalInput").ap()
    bv_in = nc.dram_tensor("bv", [1, HPC * DH], F32, kind="ExternalInput").ap()
    wo_in = nc.dram_tensor("wo", [D, D], F32, kind="ExternalInput").ap()
    bo_in = nc.dram_tensor("bo", [1, D], F32, kind="ExternalInput").ap()
    w1t_in = nc.dram_tensor("w1t", [D, DFF], F32, kind="ExternalInput").ap()
    w2t_in = nc.dram_tensor("w2t", [DFF, D], F32, kind="ExternalInput").ap()
    masks_in = nc.dram_tensor("masks", [4, 128, 512], F32, kind="ExternalInput").ap()
    identb_in = nc.dram_tensor("identb", [128, 128], BF16, kind="ExternalInput").ap()
    y_out = nc.dram_tensor("y", [OWN, D], F32, kind="ExternalOutput").ap()

    xT_sp = nc.dram_tensor("xT_sp", [2, 16, 128, T], BF16, kind=ikind).ap()
    qT_sp = nc.dram_tensor("qT_sp", [2, 16, 128, OWN], BF16, kind=ikind).ap()
    kT_sp = nc.dram_tensor("kT_sp", [2, HPC, 128, T], BF16, kind=ikind).ap()
    v_sp = nc.dram_tensor("v_sp", [2, T, HPC, 132], BF16, kind=ikind).ap()
    wo_sp = nc.dram_tensor("wo_sp", [2, 16, 128, D], BF16, kind=ikind).ap()
    tern2_sp = nc.dram_tensor("tern2_sp", [64, 128, D], BF16, kind=ikind).ap()
    xmid_sp = nc.dram_tensor("xmid_sp", [OWN, D], F32, kind=ikind).ap()
    h_sp = nc.dram_tensor("h_sp", [OWN, DFF], F32, kind=ikind).ap()
    ag_in = nc.dram_tensor("ag_in", [NQT, 2, HPC, 128, 512], BF16, kind="Internal").ap()
    ag_out = nc.dram_tensor("ag_out", [NQT, 2, 2, HPC, 128, 512], BF16, kind="Internal").ap()
    if debug:
        dbg_agin = nc.dram_tensor("dbg_agin", [NQT, 2, HPC, 128, 512], BF16,
                                  kind="ExternalOutput").ap()
        dbg_xint = nc.dram_tensor("dbg_xint", [OWN, D], BF16, kind="ExternalOutput").ap()
        dbg_wsc = nc.dram_tensor("dbg_wsc", [128, 6], F32, kind="ExternalOutput").ap()
        dbg_p = nc.dram_tensor("dbg_p", [128, 512], F32, kind="ExternalOutput").ap()
        dbg_sm = nc.dram_tensor("dbg_sm", [128, 512], F32, kind="ExternalOutput").ap()
        dbg_mt = nc.dram_tensor("dbg_mt", [128, HPC], F32, kind="ExternalOutput").ap()
        dbg_av = nc.dram_tensor("dbg_av", [128, 516], F32, kind="ExternalOutput").ap()

    AG_QT_STRIDE = 2 * 2 * HPC * 128 * 512

    with ExitStack() as stack:
        ent = stack.enter_context
        sc = Sched(nc, stack)

        masks = ent(nc.sbuf_tensor("masks_sb", [128, 4, 512], F32))
        identb = ent(nc.sbuf_tensor("identb_sb", [128, 128], BF16))
        ones_row = ent(nc.sbuf_tensor("ones_row", [1, 512], F32))
        ones_col = ent(nc.sbuf_tensor("ones_col", [128, 1], F32))
        bq_sb = ent(nc.sbuf_tensor("bq_sb", [128, 16], F32))
        bk_sb = ent(nc.sbuf_tensor("bk_sb", [128, HPC], F32))
        bv_sb = ent(nc.sbuf_tensor("bv_sb", [1, HPC * DH], F32))
        bo_sb = ent(nc.sbuf_tensor("bo_sb", [1, D], F32))
        ln1 = ent(nc.sbuf_tensor("ln1", [128, 2 * NTT], F32))
        ln2s = ent(nc.sbuf_tensor("ln2s", [128, 4 * NOT], F32))
        qs_sb = ent(nc.sbuf_tensor("qs_sb", [128, 2 * NOT], F32))
        hs_sb = ent(nc.sbuf_tensor("hs_sb", [128, 3 * NOT], F32))
        wsc = ent(nc.sbuf_tensor("wsc", [128, 6], F32))
        wacc = ent(nc.sbuf_tensor("wacc", [128, 2], F32))
        wsca = ent(nc.sbuf_tensor("wsca", [1, 16], F32))
        nmx = ent(nc.sbuf_tensor("nmx", [1, 2 * HPC + 2], F32))
        mtil = ent(nc.sbuf_tensor("mtil", [128, HPC], F32))
        stat_s1 = ent(nc.sbuf_tensor("stat_s1", [128, 1], F32))
        stat_sq = ent(nc.sbuf_tensor("stat_sq", [128, 1], F32))
        stat_wq = ent(nc.sbuf_tensor("stat_wq", [128, 1], F32))
        epsc = ent(nc.sbuf_tensor("epsc", [128, 1], F32))
        sq_scr = ent(nc.sbuf_tensor("sq_scr", [128, 2048], F32))
        tiny = ent(nc.sbuf_tensor("tiny", [128, 8], F32))

        pS0 = ent(nc.psum_tensor("pS0", [128, 512], F32))
        pS1 = ent(nc.psum_tensor("pS1", [128, 512], F32))
        pA0 = ent(nc.psum_tensor("pA0", [128, 512], F32))
        pA1 = ent(nc.psum_tensor("pA1", [128, 512], F32))
        pO0 = ent(nc.psum_tensor("pO0", [128, 512], F32))
        pO1 = ent(nc.psum_tensor("pO1", [128, 512], F32))
        pT0 = ent(nc.psum_tensor("pT0", [128, 1024], BF16))
        pT1 = ent(nc.psum_tensor("pT1", [128, 1024], BF16))
        pT = [pT0, pT1]
        pS = [pS0, pS1]
        pO = [pO0, pO1]
        pA = [pA0, pA1]

        s_ld = sc.sem("s_ld")
        s_g = sc.sem("s_g")
        s_v = sc.sem("s_v")
        s_s = sc.sem("s_s")
        s_t = sc.sem("s_t")
        s_gp = sc.sem("s_gp")
        s_cc = sc.sem("s_cc")
        s_st = sc.sem("s_st")

        try:
            _ph_try = True
            dyn = {}
            dyn_s = {}
            def _mkdyn_s(e):
                cid = e.partition_id()
                dyn_s["xtok"] = (cid % 2) * OWN
            sc.ops["sync"].append(_mkdyn_s)
            def _mkdyn(e):
                cid = e.partition_id()
                half = cid % 2
                dyn["ag"] = half * (NOJ * AG_QT_STRIDE)
                dyn["x"] = half * (OWN * D)
            sc.ops["gpsimd"].append(_mkdyn)

            for _mi in range(4):
                sc.dma("sync", masks[:, _mi, :], masks_in[_mi], "p0")
            sc.dma("sync", identb[:], identb_in, "p0")
            sc.dma("sync", bq_sb[:], bq_in, "p0")
            sc.dma("sync", bk_sb[:], bk_in, "p0")
            sc.dma("sync", bv_sb[:], bv_in, "p0")
            p0v = sc.dma("sync", bo_sb[:], bo_in, "p0")
            sc.dwait("vector", "p0", p0v)
            def init_v(e):
                e.memset(ones_row[:], 1.0)
                e.memset(epsc[:], EPS)
                e.memset(ones_col[:], 1.0)
                e.memset(wacc[:], 0.0)
                e.memset(nmx[:], 0.0)
                e.memset(ln2s[:], 0.0)
                return e.memset(hs_sb[:], 0.0)
            sc.emit("vector", init_v, inc=s_v)
            init_done = s_v.n
            sc.barrier("p0")
            if phases <= 0:
                raise _StopBuild

            with ExitStack() as ph:
                xa = ph.enter_context(nc.sbuf_tensor("xa", [128, D], F32))
                xb = ph.enter_context(nc.sbuf_tensor("xb", [128, D], F32))
                xhat = ph.enter_context(nc.sbuf_tensor("xhat", [128, D], F32))
                xhi = ph.enter_context(nc.sbuf_tensor("xhi", [128, D], BF16))
                xlo = ph.enter_context(nc.sbuf_tensor("xlo", [128, D], BF16))
                xTh_a = ph.enter_context(nc.sbuf_tensor("xTh_a", [128, 16, 512], BF16))
                xTl_a = ph.enter_context(nc.sbuf_tensor("xTl_a", [128, 16, 512], BF16))
                xTh_b = ph.enter_context(nc.sbuf_tensor("xTh_b", [128, 16, 512], BF16))
                xTl_b = ph.enter_context(nc.sbuf_tensor("xTl_b", [128, 16, 512], BF16))
                wsa = ph.enter_context(nc.sbuf_tensor("wsa", [128, 2048], F32))
                wsb = ph.enter_context(nc.sbuf_tensor("wsb", [128, 2048], F32))

                sc.wait("vector", s_v, init_done)
                sc.wait("scalar", s_v, init_done)

                wstats = []
                for r in range(16):
                    for ctile in range(4):
                        wstats.append((w1t_in[r * 128:(r + 1) * 128, ctile * 2048:(ctile + 1) * 2048], 0))
                for r in range(64):
                    wstats.append((w2t_in[r * 128:(r + 1) * 128, :], 1))
                wstats_i = 0
                wst_ld = {(p, b): 0 for p in range(2) for b in range(2)}
                wst_rd = {0: 0, 1: 0}
                wst_last_add = 0

                x_ld = {0: 0, 1: 0}
                x_rd = {0: 0, 1: 0}
                tr_ev = {0: 0, 1: 0}
                sp_done = {0: 0, 1: 0}

                n_wstats_per_tt = (len(wstats) + NTT - 1) // NTT

                for ts in range(min(NTS, K_NTS)):
                    xTh = [xTh_a, xTh_b][ts % 2]
                    xTl = [xTl_a, xTl_b][ts % 2]
                    sc.dwait("vector", f"xsp{ts % 2}", sp_done[ts % 2])
                    for j in range(4):
                        tt = ts * 4 + j
                        xt = [xa, xb][tt % 2]
                        bi = tt % 2
                        sc.wait("sync", s_s, x_rd[bi])
                        x_ld[bi] = sc.dma("sync", xt[:], x_in[tt * 128:(tt + 1) * 128, :], f"xld{bi}")
                        sc.dwait("vector", f"xld{bi}", x_ld[bi])
                        sc.emit("vector", lambda e, xt=xt: e.reduce_sum(stat_s1[:], xt[:], AX.X), inc=s_v)
                        s1v = s_v.n
                        sc.dwait("scalar", f"xld{bi}", x_ld[bi])
                        sc.emit("scalar", lambda e, xt=xt: e.activation(sq_scr[:], xt[:], AF.Square,
                                                                        accum_out=stat_sq[:]), inc=s_s)
                        sqv = s_s.n
                        sc.wait("vector", s_s, sqv)
                        def lnst1(e):
                            e.tensor_scalar(tiny[:, 0:1], stat_s1[:], 1.0 / D, 0.0, ALU.mult, ALU.add)
                            return e.tensor_scalar(tiny[:, 1:2], stat_sq[:], 1.0 / D, 0.0, ALU.mult, ALU.add)
                        sc.emit("vector", lnst1, inc=s_v)
                        sc.emit("vector", lambda e: e.tensor_tensor(tiny[:, 2:3], tiny[:, 0:1],
                                                                    tiny[:, 0:1], ALU.mult), inc=s_v)
                        sc.emit("vector", lambda e: e.tensor_tensor(tiny[:, 3:4], tiny[:, 1:2],
                                                                    tiny[:, 2:3], ALU.subtract), inc=s_v)
                        varv = s_v.n
                        sc.wait("scalar", s_v, varv)
                        sc.emit("scalar", lambda e: e.activation(tiny[:, 4:5], tiny[:, 3:4], AF.Sqrt,
                                                                 bias=epsc[:]), inc=s_s)
                        sdv = s_s.n
                        sc.wait("vector", s_s, sdv)
                        sc.emit("vector", lambda e, tt=tt: e.reciprocal(
                            ln1[:, 2 * tt:2 * tt + 1], tiny[:, 4:5]), inc=s_v)
                        sc.emit("vector", lambda e, tt=tt: e.scalar_tensor_tensor(
                            ln1[:, 2 * tt + 1:2 * tt + 2], tiny[:, 0:1], -1.0,
                            ln1[:, 2 * tt:2 * tt + 1], ALU.mult, ALU.mult), inc=s_v)
                        rsv = s_v.n
                        sc.wait("scalar", s_v, rsv)
                        sc.emit("scalar", lambda e, xt=xt, tt=tt: e.activation(
                            xhat[:], xt[:], AF.Identity, bias=ln1[:, 2 * tt + 1:2 * tt + 2],
                            scale=ln1[:, 2 * tt:2 * tt + 1]), inc=s_s)
                        x_rd[bi] = s_s.n
                        xhv = s_s.n
                        sc.wait("vector", s_s, xhv)
                        sc.emit("vector", lambda e: e.tensor_copy(xhi[:], xhat[:]), inc=s_v)
                        hiv = s_v.n
                        sc.wait("gpsimd", s_v, hiv)
                        sc.emit("gpsimd", lambda e: e.tensor_tensor(xlo[:], xhat[:], xhi[:], ALU.subtract), inc=s_gp)
                        lov = s_gp.n
                        sc.wait("tensor", s_gp, lov)
                        for dc in (range(16) if EN_TR else []):
                            bk = dc % 2
                            sc.wait("tensor", s_v, tr_ev[bk])
                            def trf(e, pb=pT[bk], dc=dc):
                                e.transpose(pb[:, 0:128], xhi[:, dc * 128:(dc + 1) * 128], identb[:])
                                return e.transpose(pb[:, 128:256], xlo[:, dc * 128:(dc + 1) * 128], identb[:])
                            sc.emit("tensor", trf, inc=s_t)
                            trv = s_t.n
                            sc.wait("vector", s_t, trv)
                            def evf(e, pb=pT[bk], dc=dc, j=j, xTh=xTh, xTl=xTl):
                                e.tensor_copy(xTh[:, dc, j * 128:(j + 1) * 128], pb[:, 0:128])
                                return e.tensor_copy(xTl[:, dc, j * 128:(j + 1) * 128], pb[:, 128:256])
                            sc.emit("vector", evf, inc=s_v)
                            tr_ev[bk] = s_v.n
                        for k in (range(n_wstats_per_tt) if EN_WS else []):
                            if wstats_i >= len(wstats):
                                break
                            src, part = wstats[wstats_i]
                            wb = wstats_i % 2
                            wt = [wsa, wsb][wb]
                            weng = "gpsimd" if part == 0 else "scalar"
                            sc.wait(weng, s_v, wst_rd[wb])
                            wst_ld[(part, wb)] = sc.dma(weng, wt[:, 0:src.shape[1]], src,
                                                        f"wst{part}{wb}")
                            sc.dwait("scalar", f"wst{part}{wb}", wst_ld[(part, wb)])
                            sc.wait("scalar", s_v, wst_last_add)
                            sc.emit("scalar", lambda e, wt=wt, n=src.shape[1]: e.activation(
                                sq_scr[:, 0:n], wt[:, 0:n], AF.Square, accum_out=stat_wq[:]), inc=s_s)
                            wsv = s_s.n
                            sc.wait("vector", s_s, wsv)
                            sc.emit("vector", lambda e, part=part: e.tensor_tensor(
                                wacc[:, part:part + 1], wacc[:, part:part + 1], stat_wq[:], ALU.add), inc=s_v)
                            wst_rd[wb] = s_v.n
                            wst_last_add = s_v.n
                            wstats_i += 1
                    spv = s_v.n
                    if not (EN_SP and EN_TR):
                        continue
                    sc.wait("gpsimd", s_v, spv)
                    sc.dma("gpsimd", xT_sp[0, :, :, ts * 512:(ts + 1) * 512].transpose([1, 0, 2]),
                           xTh[:], f"xsp{ts % 2}")
                    sp_done[ts % 2] = sc.dma("gpsimd",
                                             xT_sp[1, :, :, ts * 512:(ts + 1) * 512]
                                             .transpose([1, 0, 2]),
                                             xTl[:], f"xsp{ts % 2}")
                assert (not EN_WS) or K_NTS < 99 or wstats_i == len(wstats)

            sc.barrier("pa1")
            if phases <= 1:
                raise _StopBuild

            def wred(e):
                return e.matmul(pS0[0:1, 0:2], ones_col[:], wacc[:, 0:2], start=True, stop=True)
            sc.emit("tensor", wred, inc=s_t)
            wrv = s_t.n
            sc.wait("vector", s_t, wrv)
            def wfin(e):
                return e.tensor_scalar(wsca[0:1, 0:2], pS0[0:1, 0:2], 1.0 / (D * DFF), EPS, ALU.mult, ALU.max)
            sc.emit("vector", wfin, inc=s_v)
            wfv = s_v.n
            sc.wait("scalar", s_v, wfv)
            sc.emit("scalar", lambda e: e.activation(wsca[0:1, 2:4], wsca[0:1, 0:2], AF.Sqrt), inc=s_s)
            sq2v = s_s.n
            sc.wait("vector", s_s, sq2v)
            sc.emit("vector", lambda e: e.reciprocal(wsca[0:1, 4:6], wsca[0:1, 2:4]), inc=s_v)
            sc.emit("vector", lambda e: e.tensor_tensor(wsca[0:1, 4:6], wsca[0:1, 0:2],
                                                        wsca[0:1, 4:6], ALU.mult), inc=s_v)
            sc.emit("vector", lambda e: e.tensor_tensor(wsca[0:1, 4:6], wsca[0:1, 2:4],
                                                        wsca[0:1, 4:6], ALU.add), inc=s_v)
            sc.emit("vector", lambda e: e.tensor_scalar(wsca[0:1, 2:4], wsca[0:1, 4:6], 0.5, 0.0,
                                                        ALU.mult, ALU.add), inc=s_v)
            sc.emit("vector", lambda e: e.tensor_scalar(wsca[0:1, 4:6], wsca[0:1, 2:4], 0.5, 0.0,
                                                        ALU.mult, ALU.add), inc=s_v)
            sc.emit("vector", lambda e: e.tensor_scalar(wsca[0:1, 6:8], wsca[0:1, 4:6], -1.0, 0.0,
                                                        ALU.mult, ALU.add), inc=s_v)
            ntv = s_v.n
            sc.wait("tensor", s_v, ntv)
            def wbc(e):
                return e.matmul(pS0[:, 0:8], ones_row[0:1, 0:128], wsca[0:1, 0:8], start=True, stop=True)
            sc.emit("tensor", wbc, inc=s_t)
            wbv = s_t.n
            sc.wait("vector", s_t, wbv)
            def wscf(e):
                e.tensor_copy(wsc[:, 0:1], pS0[:, 4:5])
                e.tensor_copy(wsc[:, 1:2], pS0[:, 6:7])
                e.tensor_copy(wsc[:, 2:3], pS0[:, 5:6])
                e.tensor_copy(wsc[:, 3:4], pS0[:, 7:8])
                e.tensor_copy(wsc[:, 4:5], pS0[:, 2:3])
                return e.tensor_copy(wsc[:, 5:6], pS0[:, 3:4])
            sc.emit("vector", wscf, inc=s_v)
            wsfv = s_v.n
            if debug:
                sc.wait("sync", s_v, wsfv)
                sc.dma("sync", dbg_wsc, wsc[:], "dbgw")
            sc.wait("tensor", s_v, wsfv)

            with ExitStack() as ph:
                wf_c = [ph.enter_context(nc.sbuf_tensor(f"wf_c{i}", [128, 2, 1024], F32)) for i in range(2)]
                whi = ph.enter_context(nc.sbuf_tensor("whi", [128, 16, 1024], BF16))
                wlo = ph.enter_context(nc.sbuf_tensor("wlo", [128, 16, 1024], BF16))
                xs_h = [ph.enter_context(nc.sbuf_tensor(f"xs_h{i}", [128, 16, 512], BF16)) for i in range(2)]
                xs_l = [ph.enter_context(nc.sbuf_tensor(f"xs_l{i}", [128, 16, 512], BF16)) for i in range(2)]
                qev_h = [ph.enter_context(nc.sbuf_tensor(f"qev_h{i}", [128, 512], BF16)) for i in range(2)]
                qev_l = [ph.enter_context(nc.sbuf_tensor(f"qev_l{i}", [128, 512], BF16)) for i in range(2)]
                vev_h = [ph.enter_context(nc.sbuf_tensor(f"vev_h{i}", [128, 8, 132], BF16)) for i in range(2)]
                vev_l = [ph.enter_context(nc.sbuf_tensor(f"vev_l{i}", [128, 8, 132], BF16)) for i in range(2)]
                nsq = ph.enter_context(nc.sbuf_tensor("nsq", [128, 512], F32))

                xs_rd = {0: 0, 1: 0}
                qk_bank_rd = {0: 0, 1: 0}
                vp_bank_rd = {0: 0, 1: 0}
                qev_st = {0: 0, 1: 0}
                vev_st = {0: 0, 1: 0}
                w_rd = 0
                nsq_rd = 0
                pnorm_rd = 0

                stages = [
                    (wq_in[:, 0:1024], qT_sp, bq_sb, False, True, 0),
                    (wq_in[:, 1024:2048], qT_sp, bq_sb, False, True, 8),
                    (wk_in, kT_sp, bk_sb, False, False, 0),
                    (wv_in, v_sp, bv_sb, True, False, 0),
                ]
                for wi, (w_in, spdst, bias_sb, token_major, own_only, fcb) in enumerate(stages):
                    sc.wait("sync", s_t, w_rd)
                    wf_rd = {0: 0, 1: 0}
                    for wch in range(8):
                        wbi = wch % 2
                        sc.wait("sync", s_gp, wf_rd[wbi])
                        wldv = sc.dma("sync", wf_c[wbi][:],
                                      w_in[wch * 256:(wch + 1) * 256, :]
                                      .rearrange("(a p) f -> p a f", p=128), f"wfc{wbi}")
                        sc.dwait("vector", f"wfc{wbi}", wldv)
                        sc.wait("vector", s_t, w_rd)
                        sc.wait("gpsimd", s_t, w_rd)
                        sc.emit("vector", lambda e, b=wbi, wch=wch: e.tensor_copy(
                            whi[:, 2 * wch:2 * wch + 2, :].rearrange("p a b -> p (a b)"),
                            wf_c[b][:].rearrange("p a b -> p (a b)")), inc=s_v)
                        whiv = s_v.n
                        sc.wait("gpsimd", s_v, whiv)
                        sc.dwait("gpsimd", f"wfc{wbi}", wldv)
                        sc.emit("gpsimd", lambda e, b=wbi, wch=wch: e.tensor_tensor(
                            wlo[:, 2 * wch:2 * wch + 2, :].rearrange("p a b -> p (a b)"),
                            wf_c[b][:].rearrange("p a b -> p (a b)"),
                            whi[:, 2 * wch:2 * wch + 2, :].rearrange("p a b -> p (a b)"),
                            ALU.subtract), inc=s_gp)
                        wf_rd[wbi] = s_gp.n
                    wlov = s_gp.n
                    sc.wait("tensor", s_gp, wlov)

                    if token_major:
                        vp_bank_rd[0] = max(vp_bank_rd[0], pnorm_rd)
                        vp_bank_rd[1] = max(vp_bank_rd[1], pnorm_rd)
                    for ts in range(NTS // 2 if own_only else NTS):
                        bi = ts % 2
                        xh, xl = xs_h[bi], xs_l[bi]
                        sc.wait("sync", s_t, xs_rd[bi])
                        src0 = xT_sp[0, :, :, ts * 512:(ts + 1) * 512].transpose([1, 0, 2])
                        src1 = xT_sp[1, :, :, ts * 512:(ts + 1) * 512].transpose([1, 0, 2])
                        if own_only:
                            def _qld(e, src=src0, dst=xh, key="xts%d" % bi):
                                s2 = bass.AP(src.tensor, src.offset + dyn_s["xtok"], src.ap)
                                e.dma_start(out=dst[:], in_=s2).then_inc(
                                    sc.sem("dma_" + key).h, 16)
                            sc.sem("dma_xts%d" % bi).inc(16)
                            sc.ops["sync"].append(_qld)
                            def _qld2(e, src=src1, dst=xl, key="xts%d" % bi):
                                s2 = bass.AP(src.tensor, src.offset + dyn_s["xtok"], src.ap)
                                e.dma_start(out=dst[:], in_=s2).then_inc(
                                    sc.sem("dma_" + key).h, 16)
                            sc.sem("dma_xts%d" % bi).inc(16)
                            sc.ops["sync"].append(_qld2)
                            xldv = sc.sem("dma_xts%d" % bi).n
                        else:
                            sc.dma("sync", xh[:], src0, f"xts{bi}")
                            xldv = sc.dma("sync", xl[:], src1, f"xts{bi}")
                        sc.dwait("tensor", f"xts{bi}", xldv)
                        if not token_major:
                            for fc in range(8):
                                bk = fc % 2
                                pb = pS[bk]
                                sc.wait("tensor", s_v, qk_bank_rd[bk])
                                def qkmm(e, pb=pb, fc=fc, xh=xh, xl=xl):
                                    for dc in range(16):
                                        e.matmul(pb[:], whi[:, dc, fc * 128:(fc + 1) * 128], xh[:, dc, :],
                                                 start=(dc == 0), stop=False)
                                        e.matmul(pb[:], whi[:, dc, fc * 128:(fc + 1) * 128], xl[:, dc, :],
                                                 start=False, stop=False)
                                        ins = e.matmul(pb[:], wlo[:, dc, fc * 128:(fc + 1) * 128],
                                                       xh[:, dc, :], start=False, stop=(dc == 15))
                                    return ins
                                sc.emit("tensor", qkmm, inc=s_t)
                                mmv = s_t.n
                                xs_rd[bi] = mmv
                                w_rd = mmv
                                ebi = fc % 2
                                sc.wait("scalar", s_t, mmv)
                                sc.dwait("scalar", f"qsp{ebi}", qev_st[ebi])
                                sc.emit("scalar", lambda e, pb=pb, fc=fcb + fc, bias_sb=bias_sb,
                                        qh=qev_h[ebi]: e.activation(qh[:], pb[:], AF.Identity,
                                                                    bias=bias_sb[:, fc:fc + 1]),
                                        inc=s_s)
                                hiv = s_s.n
                                sc.wait("scalar", s_v, pnorm_rd)
                                sc.wait("scalar", s_t, nsq_rd)
                                sc.emit("scalar", lambda e, qh=qev_h[ebi]: e.activation(
                                    nsq[:], qh[:], AF.Square), inc=s_s)
                                nqv = s_s.n
                                sc.wait("vector", s_s, hiv)
                                sc.emit("vector", lambda e, pb=pb, fc=fcb + fc, bias_sb=bias_sb,
                                        qh=qev_h[ebi], ql=qev_l[ebi]: e.scalar_tensor_tensor(
                                            ql[:], pb[:], bias_sb[:, fc:fc + 1], qh[:],
                                            ALU.add, ALU.subtract), inc=s_v)
                                qk_bank_rd[bk] = s_v.n
                                sc.wait("tensor", s_s, nqv)
                                sc.wait("tensor", s_v, pnorm_rd)
                                def nmm(e):
                                    return e.matmul(pO0[0:1, 0:512], ones_col[:], nsq[:], start=True, stop=True)
                                sc.emit("tensor", nmm, inc=s_t)
                                nsq_rd = s_t.n
                                nmv = s_t.n
                                sc.wait("vector", s_t, nmv)
                                sc.emit("vector", lambda e: e.reduce_max(
                                    wsca[0:1, 8:9], pO0[0:1, 0:512], AX.X), inc=s_v)
                                sc.emit("vector", lambda e, col=(16 if own_only else fc):
                                        e.tensor_tensor(
                                            nmx[0:1, col:col + 1], nmx[0:1, col:col + 1],
                                            wsca[0:1, 8:9], ALU.max), inc=s_v)
                                pnorm_rd = s_v.n
                                lov = s_v.n
                                sc.wait("sync", s_v, qk_bank_rd[bk])
                                sc.dma("sync", spdst[0, fcb + fc, :, ts * 512:(ts + 1) * 512],
                                       qev_h[ebi][:], f"qsp{ebi}")
                                qev_st[ebi] = sc.dma("sync",
                                                     spdst[1, fcb + fc, :, ts * 512:(ts + 1) * 512],
                                                     qev_l[ebi][:], f"qsp{ebi}")
                        else:
                            for sub in range(4):
                                vbi = sub % 2
                                sc.dwait("scalar", f"vsp{vbi}", vev_st[vbi])
                                for fv in range(2):
                                    bk = fv
                                    pb = pO[bk]
                                    sc.wait("tensor", s_v, vp_bank_rd[bk])
                                    def vmm(e, pb=pb, sub=sub, fv=fv, xh=xh, xl=xl, bias_sb=bias_sb):
                                        for dc in range(16):
                                            e.matmul(pb[:], xh[:, dc, sub * 128:(sub + 1) * 128],
                                                     whi[:, dc, fv * 512:(fv + 1) * 512],
                                                     start=(dc == 0), stop=False)
                                            e.matmul(pb[:], xh[:, dc, sub * 128:(sub + 1) * 128],
                                                     wlo[:, dc, fv * 512:(fv + 1) * 512], start=False, stop=False)
                                            e.matmul(pb[:], xl[:, dc, sub * 128:(sub + 1) * 128],
                                                     whi[:, dc, fv * 512:(fv + 1) * 512], start=False, stop=False)
                                        return e.matmul(pb[:], ones_row[0:1, 0:128],
                                                 bias_sb[0:1, fv * 512:(fv + 1) * 512], start=False, stop=True)
                                    sc.emit("tensor", vmm, inc=s_t)
                                    mmv = s_t.n
                                    xs_rd[bi] = mmv
                                    w_rd = mmv
                                    sc.wait("scalar", s_t, mmv)
                                    def vevh_f(e, pb=pb, fv=fv, vh=vev_h[vbi]):
                                        return e.copy(vh[:, fv * 4:(fv + 1) * 4, 0:128],
                                               pb[:].rearrange("p (a b) -> p a b", a=4))
                                    sc.emit("scalar", vevh_f, inc=s_s)
                                    hiv = s_s.n
                                    sc.wait("vector", s_s, hiv)
                                    def vevl_f(e, pb=pb, fv=fv, vh=vev_h[vbi], vl=vev_l[vbi]):
                                        return e.tensor_tensor(vl[:, fv * 4:(fv + 1) * 4, 0:128],
                                                        pb[:].rearrange("p (a b) -> p a b", a=4),
                                                        vh[:, fv * 4:(fv + 1) * 4, 0:128], ALU.subtract)
                                    sc.emit("vector", vevl_f, inc=s_v)
                                    vp_bank_rd[bk] = s_v.n
                                def vones(e, vh=vev_h[vbi], vl=vev_l[vbi]):
                                    e.memset(vh[:, :, 128:132], 1.0)
                                    return e.memset(vl[:, :, 128:132], 0.0)
                                sc.emit("vector", vones, inc=s_v)
                                ov = s_v.n
                                sc.wait("sync", s_v, ov)
                                tt = ts * 4 + sub
                                sc.dma("sync", v_sp[0, tt * 128:(tt + 1) * 128, :, 0:132],
                                       vev_h[vbi][:], f"vsp{vbi}")
                                vev_st[vbi] = sc.dma("sync",
                                                     v_sp[1, tt * 128:(tt + 1) * 128, :, 0:132],
                                                     vev_l[vbi][:], f"vsp{vbi}")

                mxv = s_v.n
                sc.wait("vector", s_v, mxv)
                def mt1(e):
                    return e.tensor_single_scalar(wsca[0:1, 8:8 + HPC], nmx[0:1, 0:HPC],
                                                  nmx[0:1, 16:17], ALU.mult)
                sc.emit("vector", mt1, inc=s_v)
                m1v = s_v.n
                sc.wait("scalar", s_v, m1v)
                sc.emit("scalar", lambda e: e.activation(wsca[0:1, 8:8 + HPC], wsca[0:1, 8:8 + HPC],
                                                         AF.Sqrt), inc=s_s)
                m2v = s_s.n
                sc.wait("vector", s_s, m2v)
                sc.emit("vector", lambda e: e.tensor_scalar(wsca[0:1, 8:8 + HPC], wsca[0:1, 8:8 + HPC],
                                                            -C_SCALE, 0.0, ALU.mult, ALU.add), inc=s_v)
                m3v = s_v.n
                sc.wait("tensor", s_v, m3v)
                sc.wait("tensor", s_v, pnorm_rd)
                def mbc(e):
                    return e.matmul(pO0[:, 0:HPC], ones_row[0:1, 0:128], wsca[0:1, 8:8 + HPC], start=True, stop=True)
                sc.emit("tensor", mbc, inc=s_t)
                mbv = s_t.n
                sc.wait("vector", s_t, mbv)
                sc.emit("vector", lambda e: e.tensor_copy(mtil[:], pO0[:, 0:HPC]), inc=s_v)
                mtv = s_v.n
                sc.wait("tensor", s_v, mtv)

            if debug:
                sc.wait("sync", s_v, s_v.n)
                sc.dma("sync", dbg_mt, mtil[:], "dbgp")
            sc.barrier("pa2")
            if phases <= 3:
                raise _StopBuild

            with ExitStack() as ph:
                wof = [ph.enter_context(nc.sbuf_tensor(f"wof{i}", [128, D], F32)) for i in range(2)]
                wohi = [ph.enter_context(nc.sbuf_tensor(f"wohi{i}", [128, D], BF16)) for i in range(2)]
                wolo = [ph.enter_context(nc.sbuf_tensor(f"wolo{i}", [128, D], BF16)) for i in range(2)]
                t2a = [ph.enter_context(nc.sbuf_tensor(f"t2a{i}", [128, D], F32)) for i in range(2)]
                t2g = [ph.enter_context(nc.sbuf_tensor(f"t2g{i}", [128, D], F32)) for i in range(2)]
                t2b = [ph.enter_context(nc.sbuf_tensor(f"t2b{i}", [128, D], BF16)) for i in range(2)]

                wo_ld = {0: 0, 1: 0}
                wo_st = {0: 0, 1: 0}
                for dc in range(16):
                    bi = dc % 2
                    sc.dwait("sync", f"wost{bi}", wo_st[bi])
                    wo_ld[bi] = sc.dma("sync", wof[bi][:], wo_in[dc * 128:(dc + 1) * 128, :],
                                       f"wof{bi}")
                    sc.dwait("vector", f"wof{bi}", wo_ld[bi])
                    sc.emit("vector", lambda e, b=bi: e.tensor_copy(wohi[b][:], wof[b][:]), inc=s_v)
                    hv = s_v.n
                    sc.wait("gpsimd", s_v, hv)
                    sc.emit("gpsimd", lambda e, b=bi: e.tensor_tensor(wolo[b][:], wof[b][:], wohi[b][:],
                                                                      ALU.subtract), inc=s_gp)
                    gv = s_gp.n
                    sc.wait("sync", s_gp, gv)
                    sc.dma("sync", wo_sp[0, dc, :, :], wohi[bi][:], f"wost{bi}")
                    wo_st[bi] = sc.dma("sync", wo_sp[1, dc, :, :], wolo[bi][:], f"wost{bi}")

                t2_ld = {0: 0, 1: 0}
                t2_st = {0: 0, 1: 0}
                for fc in range(64):
                    bi = fc % 2
                    sc.dwait("sync", f"t2st{bi}", t2_st[bi])
                    t2_ld[bi] = sc.dma("sync", wof[bi][:], w2t_in[fc * 128:(fc + 1) * 128, :],
                                       f"wof{bi}")
                    sc.dwait("vector", f"wof{bi}", t2_ld[bi])
                    sc.emit("vector", lambda e, b=bi: e.tensor_scalar(
                        t2a[b][:], wof[b][:], wsc[:, 2:3], -1.0, ALU.is_gt, ALU.add), inc=s_v)
                    av = s_v.n
                    sc.dwait("gpsimd", f"wof{bi}", t2_ld[bi])
                    sc.wait("gpsimd", s_v, av)
                    sc.emit("gpsimd", lambda e, b=bi: e.tensor_single_scalar(
                        t2g[b][:], wof[b][:], wsc[:, 3:4], ALU.is_ge), inc=s_gp)
                    gv = s_gp.n
                    sc.wait("vector", s_gp, gv)
                    sc.emit("vector", lambda e, b=bi: e.tensor_tensor(
                        t2b[b][:], t2g[b][:], t2a[b][:], ALU.add), inc=s_v)
                    tv = s_v.n
                    sc.wait("gpsimd", s_v, tv)
                    t2_st[bi] = sc.dma("gpsimd", tern2_sp[fc, :, :], t2b[bi][:], f"t2st{bi}")

            sc.barrier("pw2")
            if phases <= 4:
                raise _StopBuild

            with ExitStack() as ph:
                kh_b = [ph.enter_context(nc.sbuf_tensor(f"kh{i}", [128, T], BF16)) for i in range(2)]
                kl_b = [ph.enter_context(nc.sbuf_tensor(f"kl{i}", [128, T], BF16)) for i in range(2)]
                vh_b = [ph.enter_context(nc.sbuf_tensor(f"vh{i}", [128, NTT, 132], BF16)) for i in range(2)]
                vl_b = [ph.enter_context(nc.sbuf_tensor(f"vl{i}", [128, NTT, 132], BF16)) for i in range(2)]
                q_h = [ph.enter_context(nc.sbuf_tensor(f"q_h{i}", [128, 512], BF16)) for i in range(2)]
                q_l = [ph.enter_context(nc.sbuf_tensor(f"q_l{i}", [128, 512], BF16)) for i in range(2)]
                pf32 = [ph.enter_context(nc.sbuf_tensor(f"pf32_{i}", [128, 512], F32)) for i in range(2)]
                smsk = [ph.enter_context(nc.sbuf_tensor(f"smsk{i}", [128, 512], F32)) for i in range(2)]
                p_hi = [ph.enter_context(nc.sbuf_tensor(f"p_hi{i}", [128, 512], BF16)) for i in range(2)]
                p_lo = [ph.enter_context(nc.sbuf_tensor(f"p_lo{i}", [128, 512], BF16)) for i in range(2)]
                attf = ph.enter_context(nc.sbuf_tensor("attf", [128, 128], F32))
                atth = ph.enter_context(nc.sbuf_tensor("atth", [128, 128], BF16))
                attl = ph.enter_context(nc.sbuf_tensor("attl", [128, 128], BF16))
                recl = ph.enter_context(nc.sbuf_tensor("recl", [128, 1], F32))
                aTh = [ph.enter_context(nc.sbuf_tensor(f"aTh{i}", [128, 512], BF16)) for i in range(2)]
                aTl = [ph.enter_context(nc.sbuf_tensor(f"aTl{i}", [128, 512], BF16)) for i in range(2)]

                pAV = [pA0, pA1, pO0, pO1]
                dbgst = ph.enter_context(nc.sbuf_tensor("dbgst", [128, 516], F32)) if debug else None
                kv_ld = {}
                kv_rd = {0: 0, 1: 0}
                q_rd = {0: 0, 1: 0}
                sc_rd_s = {0: 0, 1: 0}
                sc_rd_v = {0: 0, 1: 0}
                pf_rd_v = {0: 0, 1: 0}
                pf_rd_g = {0: 0, 1: 0}
                smsk_rd = {0: 0, 1: 0}
                p_rd = {0: 0, 1: 0}
                av_rd = {0: 0, 1: 0}
                av_wr = {0: 0, 1: 0}
                aT_st = {0: 0, 1: 0}
                att_tr = 0
                trb_rd = {0: 0, 1: 0}
                qt_store = {}

                def kv_load(qb, h, bi):
                    sc.wait("sync", s_t, kv_rd[bi])
                    sc.dma("sync", kh_b[bi][:], kT_sp[0, h, :, :], f"kv{bi}")
                    sc.dma("sync", kl_b[bi][:], kT_sp[1, h, :, :], f"kv{bi}")
                    sc.dma("sync", vh_b[bi][:], v_sp[0, :, h, :].rearrange("(a p) c -> p a c", p=128),
                           f"kv{bi}")
                    v_ = sc.dma("sync", vl_b[bi][:], v_sp[1, :, h, :].rearrange("(a p) c -> p a c", p=128),
                                f"kv{bi}")
                    kv_ld[(qb, h)] = (bi, v_)

                kv_load(0, 0, 0)
                for qb in range(2):
                    for h in range(HPC):
                        it = qb * HPC + h
                        bi = it % 2
                        kh, kl, vh, vl = kh_b[bi], kl_b[bi], vh_b[bi], vl_b[bi]
                        nqb, nh = (qb, h + 1) if h + 1 < HPC else (qb + 1, 0)
                        if nqb < 2:
                            kv_load(nqb, nh, (it + 1) % 2)
                        sc.dwait("tensor", f"kv{kv_ld[(qb, h)][0]}", kv_ld[(qb, h)][1])

                        for jqt in range(NQT // 2):
                            g = qb * (NQT // 2) + jqt
                            nkc = 4 * (g + 1)
                            qbi = g % 2
                            sc.wait("sync", s_t, q_rd[qbi])
                            r0 = h * (T // 16) + g * 32
                            qsrc0 = qT_sp[0][:, :, r0:r0 + 32].transpose([1, 0, 2])
                            qsrc1 = qT_sp[1][:, :, r0:r0 + 32].transpose([1, 0, 2])
                            sc.dma("sync", q_h[qbi][:].rearrange("p (a b) -> p a b", a=16),
                                   qsrc0, f"qld{qbi}")
                            qldv = sc.dma("sync", q_l[qbi][:].rearrange("p (a b) -> p a b", a=16),
                                          qsrc1, f"qld{qbi}")
                            sc.dwait("tensor", f"qld{qbi}", qldv)
                            for kc in range(nkc):
                                bk = kc % 2
                                pb = pS[bk]
                                sc.wait("tensor", s_s, sc_rd_s[bk])
                                sc.wait("tensor", s_v, sc_rd_v[bk])
                                def smm(e, pb=pb, kc=kc, kh=kh, kl=kl, qh=q_h[qbi], ql=q_l[qbi]):
                                    e.matmul(pb[:], kh[:, kc * 128:(kc + 1) * 128], qh[:], start=True, stop=False)
                                    e.matmul(pb[:], kh[:, kc * 128:(kc + 1) * 128], ql[:], start=False, stop=False)
                                    return e.matmul(pb[:], kl[:, kc * 128:(kc + 1) * 128], qh[:], start=False, stop=True)
                                sc.emit("tensor", smm, inc=s_t)
                                smv = s_t.n
                                q_rd[qbi] = smv
                                boundary = (kc >= 4 * g)
                                if boundary:
                                    mi = kc - 4 * g
                                    sc.wait("vector", s_t, smv)
                                    sc.wait("vector", s_s, smsk_rd[bk])
                                    sc.emit("vector", lambda e, pb=pb, mi=mi, sm=smsk[bk]: e.tensor_tensor(
                                        sm[:], pb[:], masks[:, mi, :], ALU.add), inc=s_v)
                                    sc_rd_v[bk] = s_v.n
                                    sc.wait("scalar", s_v, sc_rd_v[bk])
                                else:
                                    sc.wait("scalar", s_t, smv)
                                pfb = kc % 2
                                sc.wait("scalar", s_v, pf_rd_v[pfb])
                                sc.wait("scalar", s_gp, pf_rd_g[pfb])
                                def expf(e, pb=pb, h=h, bnd=boundary, sm=smsk[bk], pf=pf32[pfb]):
                                    src = sm[:] if bnd else pb[:]
                                    return e.activation(pf[:], src, AF.Exp, bias=mtil[:, h:h + 1], scale=C_SCALE)
                                sc.emit("scalar", expf, inc=s_s)
                                ev_ = s_s.n
                                if debug and qb == 0 and h == 0 and g == 0 and kc == 0:
                                    sc.wait("sync", s_s, ev_)
                                    sc.dma("sync", dbg_p, pf32[pfb][:], "dbgp")
                                    sc.dma("sync", dbg_sm, smsk[bk][:], "dbgp")
                                if boundary:
                                    smsk_rd[bk] = ev_
                                else:
                                    sc_rd_s[bk] = ev_
                                pbuf = kc % 2
                                sc.wait("vector", s_t, p_rd[pbuf])
                                sc.wait("vector", s_s, ev_)
                                sc.emit("vector", lambda e, pf=pf32[pfb], ph_=p_hi[pbuf]: e.tensor_copy(
                                    ph_[:], pf[:]), inc=s_v)
                                phv = s_v.n
                                pf_rd_v[pfb] = phv
                                sc.wait("gpsimd", s_t, p_rd[pbuf])
                                sc.wait("gpsimd", s_v, phv)
                                sc.emit("gpsimd", lambda e, pf=pf32[pfb], ph_=p_hi[pbuf], pl_=p_lo[pbuf]:
                                        e.tensor_tensor(pl_[:], pf[:], ph_[:], ALU.subtract), inc=s_gp)
                                plv = s_gp.n
                                pf_rd_g[pfb] = plv
                                sc.wait("tensor", s_gp, plv)
                                if kc == 0:
                                    sc.wait("tensor", s_v, av_rd[0])
                                    sc.wait("tensor", s_v, av_rd[1])
                                last_kc = (kc == nkc - 1)
                                def avmm(e, kc=kc, ph_=p_hi[pbuf], pl_=p_lo[pbuf], vh=vh, vl=vl,
                                         last=last_kc):
                                    for qs in range(4):
                                        pav = pAV[qs]
                                        e.matmul(pav[:, 0:129], ph_[:, qs * 128:(qs + 1) * 128],
                                                 vh[:, kc, 0:129], start=(kc == 0), stop=False)
                                        e.matmul(pav[:, 0:129], ph_[:, qs * 128:(qs + 1) * 128],
                                                 vl[:, kc, 0:129], start=False, stop=False)
                                        ins = e.matmul(pav[:, 0:129],
                                                       pl_[:, qs * 128:(qs + 1) * 128],
                                                       vh[:, kc, 0:129], start=False,
                                                       stop=last)
                                    return ins
                                sc.emit("tensor", avmm, inc=s_t)
                                avv = s_t.n
                                p_rd[pbuf] = avv
                                av_wr[0] = avv
                                av_wr[1] = avv
                            if debug and qb == 0 and h == 0 and g == 0:
                                sc.wait("vector", s_t, av_wr[0])
                                for _q in range(4):
                                    sc.emit("vector", lambda e, _q=_q: e.tensor_copy(
                                        dbgst[:, _q * 129:(_q + 1) * 129],
                                        pAV[_q][:, 0:129]), inc=s_v)
                                dbgavv = s_v.n
                                sc.wait("sync", s_v, dbgavv)
                                dbgdv = sc.dma("sync", dbg_av[:, 0:516], dbgst[:], "dbgp")
                                sc.dwait("vector", "dbgp", dbgdv)
                            abuf = g % 2
                            sc.dwait("vector", f"agst{abuf}", aT_st[abuf])
                            for qs in range(4):
                                pav = pAV[qs]
                                sl = 0
                                sc.wait("vector", s_t, av_wr[qs // 2])
                                sc.wait("vector", s_t, att_tr)
                                sc.emit("vector", lambda e, pav=pav, sl=sl: e.reciprocal(
                                    recl[:], pav[:, sl + 128:sl + 129]), inc=s_v)
                                sc.emit("vector", lambda e, pav=pav, sl=sl: e.tensor_scalar(
                                    attf[:], pav[:, sl:sl + 128], recl[:], 0.0,
                                    ALU.mult, ALU.add), inc=s_v)
                                sc.emit("vector", lambda e: e.tensor_copy(atth[:], attf[:]), inc=s_v)
                                aev = s_v.n
                                if qs == 3:
                                    av_rd[0] = aev
                                    av_rd[1] = aev
                                sc.wait("gpsimd", s_t, att_tr)
                                sc.wait("gpsimd", s_v, aev)
                                sc.emit("gpsimd", lambda e: e.tensor_tensor(attl[:], attf[:], atth[:],
                                                                            ALU.subtract), inc=s_gp)
                                alv = s_gp.n
                                trb = qs % 2
                                sc.wait("tensor", s_gp, alv)
                                sc.wait("tensor", s_v, trb_rd[trb])
                                def attr(e, ptb=pT[trb]):
                                    e.transpose(ptb[:, 0:128], atth[:], identb[:])
                                    return e.transpose(ptb[:, 128:256], attl[:], identb[:])
                                sc.emit("tensor", attr, inc=s_t)
                                atv = s_t.n
                                att_tr = atv
                                sc.wait("vector", s_t, atv)
                                def attTev(e, ptb=pT[trb], qs=qs, ah=aTh[abuf], al=aTl[abuf]):
                                    e.tensor_copy(ah[:, qs * 128:(qs + 1) * 128], ptb[:, 0:128])
                                    return e.tensor_copy(al[:, qs * 128:(qs + 1) * 128], ptb[:, 128:256])
                                sc.emit("vector", attTev, inc=s_v)
                                trb_rd[trb] = s_v.n
                            agv = s_v.n
                            sc.wait("sync", s_v, agv)
                            sc.dma("sync", ag_in[g, 0, h, :, :], aTh[abuf][:], f"agst{abuf}")
                            aT_st[abuf] = sc.dma("sync", ag_in[g, 1, h, :, :], aTl[abuf][:],
                                                 f"agst{abuf}")
                            qt_store[g] = (abuf, aT_st[abuf])
                        kv_rd[bi] = s_t.n
                    for jqt in range(NQT // 2):
                        g = qb * (NQT // 2) + jqt
                        sc.dwait("gpsimd", f"agst{qt_store[g][0]}", qt_store[g][1])
                        def ccf(e, g=g):
                            return e.collective_compute(
                                "AllGather", ALU.bypass, replica_groups=RG,
                                ins=[ag_in[g].opt()], outs=[ag_out[g].opt()])
                        sc.emit("gpsimd", ccf, inc=s_cc)

            sc.barrier("pb")
            if phases <= 5:
                raise _StopBuild


            with ExitStack() as ph:
                agb = ph.enter_context(nc.sbuf_tensor("agb", [128, 2, 2, HPC, 512], BF16))
                wos_h = [ph.enter_context(nc.sbuf_tensor(f"wos_h{i}", [128, 16, 512], BF16)) for i in range(2)]
                wos_l = [ph.enter_context(nc.sbuf_tensor(f"wos_l{i}", [128, 16, 512], BF16)) for i in range(2)]
                xm4 = ph.enter_context(nc.sbuf_tensor("xm4", [128, 4, D], F32))

                sc.wait("gpsimd", s_cc, s_cc.n)
                if debug:
                    dbgv = sc.dma("gpsimd", dbg_agin, ag_in, "dbgag")
                    sc.dwait("gpsimd", "dbgag", dbgv)

                ag_rd = 0
                wos_rd = {0: 0, 1: 0}
                po_rd = {0: 0, 1: 0}
                xm_st = 0
                xm_rd_v = 0
                xm_rd_s = 0

                sem_agb = sc.sem("dma_agb")
                sem_xm4 = sc.sem("dma_xm4ld")
                sem_xmsp = sc.sem("dma_xmsp")
                for j in range(NOJ):
                    sc.wait("gpsimd", s_t, ag_rd)
                    def agld(e, j=j):
                        src = ag_out[j].transpose([3, 0, 1, 2, 4])
                        src2 = bass.AP(src.tensor, src.offset + dyn["ag"], src.ap)
                        return e.dma_start(out=agb[:], in_=src2).then_inc(sem_agb.h, 16)
                    sem_agb.inc(16)
                    sc.ops["gpsimd"].append(lambda e, f=agld: f(e))
                    agldv = sem_agb.n
                    sc.wait("gpsimd", sem_xmsp, xm_st)
                    sc.wait("gpsimd", s_v, xm_rd_v)
                    sc.wait("gpsimd", s_s, xm_rd_s)
                    for qs in range(4):
                        tt = j * 4 + qs
                        def xld(e, tt=tt, qs=qs):
                            return e.dma_start(out=xm4[:, qs, :],
                                               in_=xq_in[tt * 128:(tt + 1) * 128, :]).then_inc(
                                sem_xm4.h, 16)
                        sem_xm4.inc(16)
                        sc.ops["gpsimd"].append(lambda e, f=xld: f(e))
                    xm_ld = sem_xm4.n
                    sc.wait("tensor", sem_agb, agldv)
                    for fc in range(4):
                        wbi = fc % 2
                        sc.wait("sync", s_t, wos_rd[wbi])
                        sc.dma("sync", wos_h[wbi][:],
                               wo_sp[0, :, :, fc * 512:(fc + 1) * 512].transpose([1, 0, 2]), f"wos{wbi}")
                        wldv = sc.dma("sync", wos_l[wbi][:],
                                      wo_sp[1, :, :, fc * 512:(fc + 1) * 512].transpose([1, 0, 2]),
                                      f"wos{wbi}")
                        sc.dwait("tensor", f"wos{wbi}", wldv)
                        for qs in range(4):
                            bk = qs % 2
                            pb = pO[bk]
                            sc.wait("tensor", s_v, po_rd[bk])
                            def ojmm(e, qs=qs, pb=pb, wbi=wbi, fc=fc):
                                first = True
                                for rk in range(2):
                                    for hd in range(HPC):
                                        dcp = rk * HPC + hd
                                        hi_ap = agb[:, rk, 0, hd, qs * 128:(qs + 1) * 128]
                                        lo_ap = agb[:, rk, 1, hd, qs * 128:(qs + 1) * 128]
                                        e.matmul(pb[:], hi_ap, wos_h[wbi][:, dcp, :], start=first, stop=False)
                                        first = False
                                        e.matmul(pb[:], hi_ap, wos_l[wbi][:, dcp, :], start=False, stop=False)
                                        e.matmul(pb[:], lo_ap, wos_h[wbi][:, dcp, :], start=False, stop=False)
                                return e.matmul(pb[:], ones_row[0:1, 0:128], bo_sb[0:1, fc * 512:(fc + 1) * 512],
                                         start=False, stop=True)
                            sc.emit("tensor", ojmm, inc=s_t)
                            mmv = s_t.n
                            ag_rd = mmv
                            wos_rd[wbi] = mmv
                            sc.wait("vector", s_t, mmv)
                            sc.wait("vector", sem_xm4, xm_ld)
                            sc.emit("vector", lambda e, pb=pb, qs=qs, fc=fc: e.tensor_tensor(
                                xm4[:, qs, fc * 512:(fc + 1) * 512], pb[:],
                                xm4[:, qs, fc * 512:(fc + 1) * 512], ALU.add), inc=s_v)
                            po_rd[bk] = s_v.n
                    evv = s_v.n
                    sc.wait("vector", s_v, evv)
                    sc.wait("scalar", s_v, evv)
                    for qs in range(4):
                        tt = j * 4 + qs
                        def ln2acc_v(e, qs=qs, tt=tt):
                            return e.reduce_sum(ln2s[:, tt:tt + 1], xm4[:, qs, :], AX.X)
                        sc.emit("vector", ln2acc_v, inc=s_v)
                        sc.wait("scalar", s_s, 0)
                        def ln2acc_s(e, qs=qs):
                            return e.activation(sq_scr[:], xm4[:, qs, :], AF.Square, accum_out=stat_sq[:])
                        sc.emit("scalar", ln2acc_s, inc=s_s)
                        sqv = s_s.n
                        sc.wait("vector", s_s, sqv)
                        sc.emit("vector", lambda e, tt=tt: e.tensor_copy(ln2s[:, NOT + tt:NOT + tt + 1],
                                                                         stat_sq[:]), inc=s_v)
                        stv = s_v.n
                        sc.wait("scalar", s_v, stv)
                        sc.wait("sync", s_v, stv)
                        sc.dma("sync", xmid_sp[tt * 128:(tt + 1) * 128, :], xm4[:, qs, :], sem_xmsp)
                    xm_st = sem_xmsp.n
                    xm_rd_v = s_v.n
                    xm_rd_s = s_s.n
                    sc.wait("sync", s_v, xm_rd_v)

            sc.barrier("poj")
            if phases <= 6:
                raise _StopBuild

            phX = ExitStack()
            xiT = phX.enter_context(nc.sbuf_tensor("xiT", [128, 16, OWN], BF16))
            with ExitStack() as ph:
                xm_l = [ph.enter_context(nc.sbuf_tensor(f"xm_l{i}", [128, D], F32)) for i in range(2)]
                xh2 = ph.enter_context(nc.sbuf_tensor("xh2", [128, D], F32))
                qt_i = ph.enter_context(nc.sbuf_tensor("qt_i", [128, D], I32))
                qt_y = ph.enter_context(nc.sbuf_tensor("qt_y", [128, D], F32))
                qt_d = ph.enter_context(nc.sbuf_tensor("qt_d", [128, D], F32))
                qt_c = ph.enter_context(nc.sbuf_tensor("qt_c", [128, D], F32))
                qt_r = ph.enter_context(nc.sbuf_tensor("qt_r", [128, D], F32))
                x_int = ph.enter_context(nc.sbuf_tensor("x_int", [128, D], BF16))

                for tt in range(NOT):
                    def ln2f_v(e, tt=tt):
                        e.tensor_scalar(tiny[:, 0:1], ln2s[:, tt:tt + 1], 1.0 / D, 0.0, ALU.mult, ALU.add)
                        return e.tensor_scalar(tiny[:, 1:2], ln2s[:, NOT + tt:NOT + tt + 1], 1.0 / D, 0.0,
                                               ALU.mult, ALU.add)
                    sc.emit("vector", ln2f_v, inc=s_v)
                    sc.emit("vector", lambda e: e.tensor_tensor(tiny[:, 2:3], tiny[:, 0:1],
                                                                tiny[:, 0:1], ALU.mult), inc=s_v)
                    sc.emit("vector", lambda e: e.tensor_tensor(tiny[:, 3:4], tiny[:, 1:2],
                                                                tiny[:, 2:3], ALU.subtract), inc=s_v)
                    vv = s_v.n
                    sc.wait("scalar", s_v, vv)
                    sc.emit("scalar", lambda e: e.activation(tiny[:, 4:5], tiny[:, 3:4], AF.Sqrt,
                                                             bias=epsc[:]), inc=s_s)
                    sv = s_s.n
                    sc.wait("vector", s_s, sv)
                    sc.emit("vector", lambda e, tt=tt: e.reciprocal(
                        ln2s[:, 2 * NOT + tt:2 * NOT + tt + 1], tiny[:, 4:5]), inc=s_v)
                    sc.emit("vector", lambda e, tt=tt: e.scalar_tensor_tensor(
                        ln2s[:, 3 * NOT + tt:3 * NOT + tt + 1], tiny[:, 0:1], -1.0,
                        ln2s[:, 2 * NOT + tt:2 * NOT + tt + 1], ALU.mult, ALU.mult), inc=s_v)
                ln2fin = s_v.n

                xm_rd = {0: 0, 1: 0}
                trb_rd2 = {0: 0, 1: 0}
                xh2_rd_v = 0
                xh2_rd_g = 0
                for tt in range(NOT):
                    bi = tt % 2
                    sc.wait("sync", s_s, xm_rd[bi])
                    xldv = sc.dma("sync", xm_l[bi][:], xmid_sp[tt * 128:(tt + 1) * 128, :], f"xml{bi}")
                    sc.dwait("scalar", f"xml{bi}", xldv)
                    sc.wait("scalar", s_v, ln2fin)
                    sc.wait("scalar", s_v, xh2_rd_v)
                    sc.wait("scalar", s_gp, xh2_rd_g)
                    sc.emit("scalar", lambda e, b=bi, tt=tt: e.activation(
                        xh2[:], xm_l[b][:], AF.Identity, bias=ln2s[:, 3 * NOT + tt:3 * NOT + tt + 1],
                        scale=ln2s[:, 2 * NOT + tt:2 * NOT + tt + 1]), inc=s_s)
                    xm_rd[bi] = s_s.n
                    xhv = s_s.n
                    sc.wait("vector", s_s, xhv)
                    sc.emit("vector", lambda e: e.tensor_reduce(tiny[:, 0:1], xh2[:], AX.X,
                                                                ALU.max, True), inc=s_v)
                    sc.emit("vector", lambda e: e.tensor_scalar(tiny[:, 0:1], tiny[:, 0:1], EPS,
                                                                None, ALU.max), inc=s_v)
                    sc.emit("vector", lambda e: e.reciprocal(tiny[:, 1:2], tiny[:, 0:1]), inc=s_v)
                    sc.emit("vector", lambda e, tt=tt: e.tensor_scalar(
                        qs_sb[:, tt:tt + 1], tiny[:, 1:2], 127.0, 0.0, ALU.mult, ALU.add), inc=s_v)
                    sc.emit("vector", lambda e: e.tensor_scalar(
                        tiny[:, 2:3], tiny[:, 0:1], 1.0 / 127.0, 0.0, ALU.mult, ALU.add), inc=s_v)
                    sc.emit("vector", lambda e, tt=tt: e.tensor_tensor(
                        qs_sb[:, NOT + tt:NOT + tt + 1], tiny[:, 2:3], wsc[:, 4:5], ALU.mult), inc=s_v)
                    qsv = s_v.n
                    xh2_rd_v = qsv
                    sc.wait("gpsimd", s_v, qsv)
                    sc.wait("gpsimd", s_s, xhv)
                    sc.emit("gpsimd", lambda e, tt=tt: e.tensor_scalar(
                        qt_y[:], xh2[:], qs_sb[:, tt:tt + 1], 0.0, ALU.mult, ALU.add), inc=s_gp)
                    yv = s_gp.n
                    xh2_rd_g = yv
                    sc.wait("vector", s_gp, yv)
                    sc.emit("vector", lambda e: e.tensor_copy(qt_i[:], qt_y[:]), inc=s_v)
                    i1v = s_v.n
                    sc.emit("vector", lambda e: e.tensor_tensor(qt_d[:], qt_y[:], qt_i[:],
                                                                ALU.subtract), inc=s_v)
                    dv = s_v.n
                    sc.wait("gpsimd", s_v, dv)
                    sc.emit("gpsimd", lambda e: e.tensor_scalar(qt_c[:], qt_d[:], 0.5, -1.0,
                                                                ALU.is_gt, ALU.add), inc=s_gp)
                    sc.emit("gpsimd", lambda e: e.tensor_single_scalar(qt_r[:], qt_d[:], -0.5,
                                                                       ALU.is_ge), inc=s_gp)
                    sc.emit("gpsimd", lambda e: e.tensor_tensor(qt_c[:], qt_r[:], qt_c[:],
                                                                ALU.add), inc=s_gp)
                    corv = s_gp.n
                    sc.wait("vector", s_gp, corv)
                    sc.emit("vector", lambda e: e.tensor_tensor(qt_d[:], qt_i[:], qt_c[:], ALU.add), inc=s_v)
                    resv = s_v.n
                    sc.emit("vector", lambda e: e.tensor_scalar(x_int[:], qt_d[:], 127.0, -128.0,
                                                                ALU.min, ALU.max), inc=s_v)
                    xiv = s_v.n
                    if debug:
                        sc.wait("sync", s_v, xiv)
                        sc.dma("sync", dbg_xint[tt * 128:(tt + 1) * 128, :], x_int[:], "dbgxi")
                    sc.wait("tensor", s_v, xiv)
                    for dc in range(16):
                        trb = dc % 2
                        sc.wait("tensor", s_v, trb_rd2[trb])
                        def xtr(e, ptb=pT[trb], dc=dc):
                            return e.transpose(ptb[:, 0:128], x_int[:, dc * 128:(dc + 1) * 128], identb[:])
                        sc.emit("tensor", xtr, inc=s_t)
                        tv = s_t.n
                        sc.wait("vector", s_t, tv)
                        sc.emit("vector", lambda e, ptb=pT[trb], dc=dc, tt=tt: e.tensor_copy(
                            xiT[:, dc, tt * 128:(tt + 1) * 128], ptb[:, 0:128]), inc=s_v)
                        trb_rd2[trb] = s_v.n
                xiT_done = s_v.n

            sc.barrier("pd0")
            if phases <= 7:
                raise _StopBuild

            with ExitStack() as ph:
                w1s = [ph.enter_context(nc.sbuf_tensor(f"w1s{i}", [128, 2, 1024], F32)) for i in range(2)]
                t1a = ph.enter_context(nc.sbuf_tensor("t1a", [128, 2048], F32))
                t1g = ph.enter_context(nc.sbuf_tensor("t1g", [128, 2048], F32))
                tern1 = [ph.enter_context(nc.sbuf_tensor(f"tern1_{i}", [128, 16, 1024], BF16)) for i in range(2)]
                hev = [ph.enter_context(nc.sbuf_tensor(f"hev{i}", [128, 512], F32)) for i in range(2)]

                sc.wait("tensor", s_v, xiT_done)

                w1_rd = {0: 0, 1: 0}
                w1v_rd = {0: 0, 1: 0}
                t1_rd = {0: 0, 1: 0}
                t1a_rd = 0
                t1g_rd = 0
                hev_st = {0: 0, 1: 0}
                habs_rd = {0: 0, 1: 0}
                pd1_rd = {0: 0, 1: 0, 2: 0, 3: 0}
                pD = [pO0, pO1, pS0, pS1]

                for fg in range(8):
                    tbi = fg % 2
                    sc.wait("gpsimd", s_t, t1_rd[tbi])
                    for ch in range(8):
                        wbi = ch % 2
                        sc.wait("sync", s_gp, w1_rd[wbi])
                        sc.wait("sync", s_v, w1v_rd[wbi])
                        ldv = sc.dma("sync", w1s[wbi][:],
                                     w1t_in[ch * 256:(ch + 1) * 256, fg * 1024:(fg + 1) * 1024]
                                     .rearrange("(a p) f -> p a f", p=128), f"w1s{wbi}")
                        sc.dwait("vector", f"w1s{wbi}", ldv)
                        sc.wait("vector", s_v, t1a_rd)
                        sc.emit("vector", lambda e, b=wbi: e.tensor_scalar(
                            t1a[:], w1s[b][:].rearrange("p a b -> p (a b)"), wsc[:, 0:1], -1.0,
                            ALU.is_gt, ALU.add), inc=s_v)
                        av = s_v.n
                        sc.wait("gpsimd", s_v, t1g_rd)
                        sc.dwait("gpsimd", f"w1s{wbi}", ldv)
                        sc.emit("gpsimd", lambda e, b=wbi: e.tensor_single_scalar(
                            t1g[:], w1s[b][:].rearrange("p a b -> p (a b)"), wsc[:, 1:2],
                            ALU.is_ge), inc=s_gp)
                        gv = s_gp.n
                        w1_rd[wbi] = gv
                        sc.wait("vector", s_gp, gv)
                        sc.wait("vector", s_v, av)
                        def ternf(e, ch=ch, tbi=tbi):
                            dst = tern1[tbi][:, 2 * ch:2 * ch + 2, :].rearrange("p a b -> p (a b)")
                            return e.tensor_tensor(dst, t1g[:], t1a[:], ALU.add)
                        sc.emit("vector", ternf, inc=s_v)
                        t1a_rd = s_v.n
                        t1g_rd = s_v.n
                        w1v_rd[wbi] = av
                    t1v = s_v.n
                    sc.wait("tensor", s_v, t1v)
                    for tt in range(NOT):
                        for fi in range(2):
                            b = (tt * 2 + fi) % 4
                            pb = pD[b]
                            sc.wait("tensor", s_s, pd1_rd[b])
                            def mm1(e, pb=pb, tt=tt, fi=fi, tbi=tbi):
                                for dc in range(16):
                                    ins = e.matmul(pb[:], xiT[:, dc, tt * 128:(tt + 1) * 128],
                                                   tern1[tbi][:, dc, fi * 512:(fi + 1) * 512],
                                                   start=(dc == 0), stop=(dc == 15))
                                return ins
                            sc.emit("tensor", mm1, inc=s_t)
                            mmv = s_t.n
                            t1_rd[tbi] = mmv
                            hbi = fi
                            sc.wait("scalar", s_t, mmv)
                            sc.dwait("scalar", f"hsp{hbi}", hev_st[hbi])
                            sc.wait("scalar", s_v, habs_rd[hbi])
                            def gel(e, pb=pb, tt=tt, b=hbi):
                                return e.activation(hev[b][:], pb[:], AF.Gelu,
                                             scale=qs_sb[:, NOT + tt:NOT + tt + 1])
                            sc.emit("scalar", gel, inc=s_s)
                            pd1_rd[b] = s_s.n
                            gv = s_s.n
                            sc.wait("vector", s_s, gv)
                            sc.emit("vector", lambda e, b=hbi: e.tensor_reduce(
                                tiny[:, 0:1], hev[b][:], AX.X, ALU.max, True), inc=s_v)
                            sc.emit("vector", lambda e, tt=tt: e.tensor_tensor(
                                hs_sb[:, tt:tt + 1], hs_sb[:, tt:tt + 1], tiny[:, 0:1],
                                ALU.max), inc=s_v)
                            habs_rd[hbi] = s_v.n
                            hv = s_v.n
                            sc.wait("gpsimd", s_v, hv)
                            hev_st[hbi] = sc.dma("gpsimd",
                                                 h_sp[tt * 128:(tt + 1) * 128,
                                                      fg * 1024 + fi * 512:fg * 1024 + (fi + 1) * 512],
                                                 hev[hbi][:], f"hsp{hbi}")

            phX.close()
            sc.barrier("pd1")
            if phases <= 8:
                raise _StopBuild

            with ExitStack() as ph:
                hT = ph.enter_context(nc.sbuf_tensor("hT", [128, 64, 512], BF16))
                hld = [ph.enter_context(nc.sbuf_tensor(f"hld{i}", [128, DFF // 2], F32)) for i in range(2)]
                q2_i = ph.enter_context(nc.sbuf_tensor("q2_i", [128, 2048], I32))
                q2_y = ph.enter_context(nc.sbuf_tensor("q2_y", [128, 2048], F32))
                q2_d = ph.enter_context(nc.sbuf_tensor("q2_d", [128, 2048], F32))
                q2_c = ph.enter_context(nc.sbuf_tensor("q2_c", [128, 2048], F32))
                h_int = ph.enter_context(nc.sbuf_tensor("h_int", [128, 2048], BF16))
                t2s = [ph.enter_context(nc.sbuf_tensor(f"t2s{i}", [128, 16, 512], BF16)) for i in range(2)]
                yst = [ph.enter_context(nc.sbuf_tensor(f"yst{i}", [128, 512], F32)) for i in range(2)]
                xms = [ph.enter_context(nc.sbuf_tensor(f"xms{i}", [128, 512], F32)) for i in range(2)]

                for tt in range(NOT):
                    sc.emit("vector", lambda e, tt=tt: e.tensor_scalar(
                        tiny[:, 0:1], hs_sb[:, tt:tt + 1], EPS, None, ALU.max), inc=s_v)
                    sc.emit("vector", lambda e: e.reciprocal(tiny[:, 1:2], tiny[:, 0:1]), inc=s_v)
                    sc.emit("vector", lambda e, tt=tt: e.tensor_scalar(
                        hs_sb[:, NOT + tt:NOT + tt + 1], tiny[:, 1:2], 127.0, 0.0,
                        ALU.mult, ALU.add), inc=s_v)
                    sc.emit("vector", lambda e: e.tensor_scalar(
                        tiny[:, 2:3], tiny[:, 0:1], 1.0 / 127.0, 0.0, ALU.mult, ALU.add), inc=s_v)
                    sc.emit("vector", lambda e, tt=tt: e.tensor_tensor(
                        hs_sb[:, 2 * NOT + tt:2 * NOT + tt + 1], tiny[:, 2:3],
                        wsc[:, 5:6], ALU.mult), inc=s_v)
                hscv = s_v.n
                sc.wait("scalar", s_v, hscv)
                sc.wait("gpsimd", s_v, hscv)

                h_rd = {0: 0, 1: 0}
                trb_rd3 = {0: 0, 1: 0}
                hT_rd = 0
                t2_rd = {0: 0, 1: 0}
                po2_rd = {0: 0, 1: 0, 2: 0, 3: 0}
                mm2_done = {0: 0, 1: 0, 2: 0, 3: 0}
                xms_rd = {0: 0, 1: 0}
                yst_rd = {0: 0, 1: 0}
                pD = [pO0, pO1, pS0, pS1]

                for tc in range(NOT // 4):
                    sc.wait("tensor", s_t, hT_rd)
                    for tloc in range(4):
                        tt = tc * 4 + tloc
                        for half in range(2):
                            hbi = half
                            sc.wait("sync", s_gp, h_rd[hbi])
                            hldv = sc.dma("sync", hld[hbi][:],
                                          h_sp[tt * 128:(tt + 1) * 128,
                                               half * (DFF // 2):(half + 1) * (DFF // 2)], f"hld{hbi}")
                            for ch in range(2):
                                cs = half * (DFF // 2) + ch * 2048
                                src = hld[hbi][:, ch * 2048:(ch + 1) * 2048]
                                sc.dwait("gpsimd", f"hld{hbi}", hldv)
                                sc.emit("gpsimd", lambda e, src=src, tt=tt: e.tensor_scalar(
                                    q2_y[:], src, hs_sb[:, NOT + tt:NOT + tt + 1], 0.0,
                                    ALU.mult, ALU.add), inc=s_gp)
                                yv = s_gp.n
                                h_rd[hbi] = yv
                                sc.wait("vector", s_gp, yv)
                                sc.emit("vector", lambda e: e.tensor_copy(q2_i[:], q2_y[:]), inc=s_v)
                                sc.emit("vector", lambda e: e.tensor_tensor(q2_d[:], q2_y[:], q2_i[:],
                                                                            ALU.subtract), inc=s_v)
                                dv = s_v.n
                                sc.wait("gpsimd", s_v, dv)
                                sc.emit("gpsimd", lambda e: e.tensor_scalar(q2_c[:], q2_d[:], 0.5, -1.0,
                                                                            ALU.is_gt, ALU.add), inc=s_gp)
                                sc.emit("gpsimd", lambda e: e.tensor_single_scalar(
                                    q2_y[:], q2_d[:], -0.5, ALU.is_ge), inc=s_gp)
                                sc.emit("gpsimd", lambda e: e.tensor_tensor(q2_c[:], q2_y[:], q2_c[:],
                                                                            ALU.add), inc=s_gp)
                                corv = s_gp.n
                                sc.wait("vector", s_gp, corv)
                                sc.emit("vector", lambda e: e.tensor_tensor(q2_d[:], q2_i[:], q2_c[:],
                                                                            ALU.add), inc=s_v)
                                sc.emit("vector", lambda e: e.tensor_scalar(h_int[:], q2_d[:], 127.0,
                                                                            -128.0, ALU.min, ALU.max),
                                        inc=s_v)
                                hiv = s_v.n
                                sc.wait("tensor", s_v, hiv)
                                for dcl in range(16):
                                    fc = cs // 128 + dcl
                                    trb = dcl % 2
                                    sc.wait("tensor", s_v, trb_rd3[trb])
                                    def htr(e, ptb=pT[trb], dcl=dcl):
                                        return e.transpose(ptb[:, 0:128], h_int[:, dcl * 128:(dcl + 1) * 128],
                                                    identb[:])
                                    sc.emit("tensor", htr, inc=s_t)
                                    tv = s_t.n
                                    sc.wait("vector", s_t, tv)
                                    sc.emit("vector", lambda e, ptb=pT[trb], fc=fc, tloc=tloc:
                                            e.tensor_copy(hT[:, fc, tloc * 128:(tloc + 1) * 128],
                                                          ptb[:, 0:128]), inc=s_v)
                                    trb_rd3[trb] = s_v.n
                    hTv = s_v.n
                    sc.wait("tensor", s_v, hTv)
                    for ec in range(4):
                        for fcg in range(4):
                            t2bi = fcg % 2
                            sc.wait("scalar", s_t, t2_rd[t2bi])
                            t2ldv = sc.dma("scalar", t2s[t2bi][:],
                                           tern2_sp[fcg * 16:(fcg + 1) * 16, :, ec * 512:(ec + 1) * 512]
                                           .transpose([1, 0, 2]), f"t2s{t2bi}")
                            sc.dwait("tensor", f"t2s{t2bi}", t2ldv)
                            for tloc in range(4):
                                b = tloc
                                pb = pD[b]
                                if fcg == 0:
                                    sc.wait("tensor", s_v, po2_rd[b])
                                def mm2(e, pb=pb, tloc=tloc, fcg=fcg, t2bi=t2bi):
                                    for fl in range(16):
                                        fc = fcg * 16 + fl
                                        ins = e.matmul(pb[:], hT[:, fc, tloc * 128:(tloc + 1) * 128],
                                                       t2s[t2bi][:, fl, :],
                                                       start=(fcg == 0 and fl == 0),
                                                       stop=(fcg == 3 and fl == 15))
                                    return ins
                                sc.emit("tensor", mm2, inc=s_t)
                                t2_rd[t2bi] = s_t.n
                                hT_rd = s_t.n
                                mm2_done[b] = s_t.n
                        for tloc in range(4):
                            tt = tc * 4 + tloc
                            b = tloc
                            pb = pD[b]
                            xbi = tloc % 2
                            ybi = tloc % 2
                            sc.wait("gpsimd", s_v, xms_rd[xbi])
                            xmldv = sc.dma("gpsimd", xms[xbi][:],
                                           xmid_sp[tt * 128:(tt + 1) * 128, ec * 512:(ec + 1) * 512],
                                           f"xms{xbi}")
                            sc.wait("vector", s_t, mm2_done[b])
                            sc.dwait("vector", f"xms{xbi}", xmldv)
                            sc.dwait("vector", f"yst{ybi}", yst_rd[ybi])
                            def yev(e, pb=pb, tt=tt, xbi=xbi, ybi=ybi):
                                return e.scalar_tensor_tensor(yst[ybi][:], pb[:],
                                                       hs_sb[:, 2 * NOT + tt:2 * NOT + tt + 1],
                                                       xms[xbi][:], ALU.mult, ALU.add)
                            sc.emit("vector", yev, inc=s_v)
                            po2_rd[b] = s_v.n
                            xms_rd[xbi] = s_v.n
                            yv = s_v.n
                            sc.wait("gpsimd", s_v, yv)
                            yst_rd[ybi] = sc.dma("gpsimd",
                                                 y_out[tt * 128:(tt + 1) * 128, ec * 512:(ec + 1) * 512],
                                                 yst[ybi][:], f"yst{ybi}")

            sc.dwait("sync", "yst0", sc.sem("dma_yst0").n)
            sc.dwait("sync", "yst1", sc.sem("dma_yst1").n)
            sc.dwait("gpsimd", "yst0", sc.sem("dma_yst0").n)
            sc.dwait("gpsimd", "yst1", sc.sem("dma_yst1").n)


        except _StopBuild:
            pass

        with nc.Block() as block:
            def mk(name):
                def run(e):
                    for op in sc.ops[name]:
                        op(e)
                return run
            block.sync(mk("sync"))
            block.tensor(mk("tensor"))
            block.vector(mk("vector"))
            block.scalar(mk("scalar"))
            block.gpsimd(mk("gpsimd"))

    return nc


def _build_masks():
    j = np.arange(512)
    tloc = (j % 32) * 16 + j // 32
    k = np.arange(128)[:, None]
    out = np.zeros((4, 128, 512), dtype=np.float32)
    for i in range(4):
        out[i] = np.where(k + 128 * i <= tloc[None, :], 0.0, NEG).astype(np.float32)
    return out


_NC_CACHE = {}


def _get_nc(T, debug=False, phases=99):
    key = (T, debug, phases)
    if key not in _NC_CACHE:
        _NC_CACHE[key] = build_program(T=T, debug=debug, phases=phases)
    return _NC_CACHE[key]


def kernel(L0_attn_x, L0_attn_Wq, L0_attn_bq, L0_attn_Wk, L0_attn_bk,
           L0_attn_Wv, L0_attn_bv, L0_attn_Wo, L0_attn_bo, L0_ffn_w1, L0_ffn_w2,
           _T=None, _debug=False, _phases=99):
    import ml_dtypes
    x = np.ascontiguousarray(np.asarray(L0_attn_x, dtype=np.float32))
    Bsz, Tfull, Dm = x.shape
    T = _T or Tfull
    wq = np.asarray(L0_attn_Wq, dtype=np.float32)
    wk = np.asarray(L0_attn_Wk, dtype=np.float32)
    wv = np.asarray(L0_attn_Wv, dtype=np.float32)
    wo = np.ascontiguousarray(np.asarray(L0_attn_Wo, dtype=np.float32))
    bq = np.asarray(L0_attn_bq, dtype=np.float32)
    bk = np.asarray(L0_attn_bk, dtype=np.float32)
    bv = np.asarray(L0_attn_bv, dtype=np.float32)
    bo = np.asarray(L0_attn_bo, dtype=np.float32)
    w1t = np.ascontiguousarray(np.asarray(L0_ffn_w1, dtype=np.float32).T)
    w2t = np.ascontiguousarray(np.asarray(L0_ffn_w2, dtype=np.float32).T)
    masks = _build_masks()
    identb = np.eye(128, dtype=np.float32).astype(ml_dtypes.bfloat16)

    nc = _get_nc(T, _debug, _phases)
    in_maps = []
    OWN_ = T // 2
    l = np.arange(OWN_)
    for c in range(8):
        p, half = c // 2, c % 2
        g_l = half * (T // 1024) + l // 512
        jj = l % 512
        tprime = g_l * 512 + (jj % 32) * 16 + jj // 32
        hsl = slice(half * HPC * DH, (half + 1) * HPC * DH)
        im = {
            "x": np.ascontiguousarray(x[p, :T, :]),
            "xq": np.ascontiguousarray(x[p, tprime, :]),
            "wq": wq,
            "wk": np.ascontiguousarray(wk[:, hsl]),
            "wv": np.ascontiguousarray(wv[:, hsl]),
            "bq": np.ascontiguousarray(bq.reshape(16, 128).T),
            "bk": np.ascontiguousarray(bk[hsl].reshape(HPC, 128).T),
            "bv": np.ascontiguousarray(bv[hsl].reshape(1, HPC * DH)),
            "wo": wo,
            "bo": np.ascontiguousarray(bo.reshape(1, D)),
            "w1t": w1t,
            "w2t": w2t,
            "masks": masks,
            "identb": identb,
        }
        in_maps.append(im)

    res = run_bass_kernel_spmd(nc, in_maps, core_ids=list(range(8)))
    out = np.empty((Bsz, T, Dm), dtype=np.float32)
    for c in range(8):
        p, half = c // 2, c % 2
        g_l = half * (T // 1024) + l // 512
        jj = l % 512
        tprime = g_l * 512 + (jj % 32) * 16 + jj // 32
        out[p, tprime, :] = np.asarray(res.results[c]["y"]).reshape(T // 2, D)
    if _debug:
        return out, res
    if T != Tfull:
        full = np.zeros((Bsz, Tfull, Dm), dtype=np.float32)
        full[:, :T, :] = out
        return full
    return out
